# version 38
# baseline (speedup 1.0000x reference)
import numpy as np
import concourse.bass as bass
import concourse.mybir as mybir
from concourse.bass_utils import run_bass_kernel_spmd

F32 = mybir.dt.float32
BF16 = mybir.dt.bfloat16
ALU = mybir.AluOpType
ACT = mybir.ActivationFunctionType

B, N, C = 8, 4096, 1024
ST = 256
NST = N // ST
NCH = C // 128
LN_EPS = 1e-5
ATT_EPS = 1e-15


def build(debug=False):
    nc = bass.Bass("TRN2", target_bir_lowering=False, detect_race_conditions=False)

    x_st = nc.dram_tensor("x_st", [NST, 128, NCH * ST], F32, kind="ExternalInput").ap()
    w_r_d = nc.dram_tensor("w_r_d", [128, NCH * 1024], F32, kind="ExternalInput").ap()
    w_k_d = nc.dram_tensor("w_k_d", [128, NCH * 1024], F32, kind="ExternalInput").ap()
    w_v_d = nc.dram_tensor("w_v_d", [128, NCH * 1024], F32, kind="ExternalInput").ap()
    w_qkv_d = nc.dram_tensor("w_qkv_d", [128, NCH * 3072], F32, kind="ExternalInput").ap()
    w_p_d = nc.dram_tensor("w_p_d", [128, NCH * 1024], F32, kind="ExternalInput").ap()
    mu_r_d = nc.dram_tensor("mu_r_d", [128, NCH], F32, kind="ExternalInput").ap()
    mu_k_d = nc.dram_tensor("mu_k_d", [128, NCH], F32, kind="ExternalInput").ap()
    bias_d = nc.dram_tensor("bias_d", [128, NCH], F32, kind="ExternalInput").ap()
    bmask_d = nc.dram_tensor("bmask_d", [128, 128], F32, kind="ExternalInput").ap()

    sk = "ExternalOutput" if debug else "Internal"
    u_scr = nc.dram_tensor("u_scr", [NST, 128, NCH * ST], F32, kind=sk).ap()
    q_scr = nc.dram_tensor("q_scr", [NST, 128, NCH * ST], BF16, kind=sk).ap()
    z_out = nc.dram_tensor("z_out", [NST, 128, NCH * ST], F32, kind="ExternalOutput").ap()

    from contextlib import ExitStack
    with ExitStack() as _es:
        def SB(name, shape, dt):
            return _es.enter_context(nc.sbuf_tensor(name, shape, dt))

        def PSUM(name, shape):
            return _es.enter_context(nc.psum_tensor(name, shape, F32))

        w_r = SB("w_r", [128, NCH * 1024], BF16)
        w_k = SB("w_k", [128, NCH * 1024], BF16)
        w_v = SB("w_v", [128, NCH * 1024], BF16)
        wsp = SB("wsp", [128, NCH * 3072], BF16)
        xbuf = [SB("xbuf0", [128, NCH * ST], F32), SB("xbuf1", [128, NCH * ST], F32)]
        ubuf = SB("ubuf", [128, NCH * ST], F32)
        wbuf = SB("wbuf", [128, NCH * ST], F32)
        xh = SB("xh", [128, NCH * 260], F32)
        scr = SB("scr", [128, NCH * ST], BF16)
        rbuf = SB("rbuf", [128, NCH * ST], BF16)
        vcp = SB("vcp", [128, NCH * ST], BF16)
        xrb = [SB("xr0", [128, NCH * ST], BF16), SB("xr1", [128, NCH * ST], BF16)]
        xkb = [SB("xk0", [128, NCH * ST], BF16), SB("xk1", [128, NCH * ST], BF16)]
        xb16 = SB("xb16", [128, NCH * ST], BF16)
        xsq = SB("xsq", [128, NCH * ST], BF16)
        kact = SB("kact", [128, NCH * ST], BF16)
        bfA = [SB("bfA0", [128, NCH * ST], BF16), SB("bfA1", [128, NCH * ST], BF16)]
        qbf = [SB("qb0", [128, NCH * ST], BF16), SB("qb1", [128, NCH * ST], BF16)]
        ktm = SB("ktm", [128, 1024], BF16)
        vpad = SB("vpad", [128, 1056], BF16)
        mu_r_s = SB("mu_r_s", [128, NCH], F32)
        mu_k_s = SB("mu_k_s", [128, NCH], F32)
        bias_s = SB("bias_s", [128, NCH], F32)
        bmask = SB("bmask", [128, 128], BF16)
        ones_l = SB("ones_l", [128, 128], BF16)
        ones_row = SB("ones_row", [1, 128], BF16)
        epsA_row = SB("epsA_row", [1, ST], BF16)
        cst = SB("cst", [128, 5], F32)
        ts1 = SB("ts1", [128, ST], F32)
        ts2 = SB("ts2", [128, ST], F32)
        drec = SB("drec", [128, 2 * ST], F32)
        rs_s = SB("rs_s", [128, ST], F32)
        m_s = SB("m_s", [128, ST], F32)
        s_app = SB("s_app", [128, 8 * 128], BF16)
        denrep = SB("denrep", [128, 8 * 128], BF16)
        dv = SB("dv", [128, 8], F32)

        ps_st = [PSUM("ps_st0", [128, 512]), PSUM("ps_st1", [128, 512])]
        ps_rot = [PSUM("ps_rot0", [128, 512]), PSUM("ps_rot1", [128, 512])]
        ps_s = [PSUM("ps_s0", [128, 512]), PSUM("ps_s1", [128, 512]), PSUM("ps_s2", [128, 512])]
        ps_x = PSUM("ps_x", [128, 512])

        sem_x = _es.enter_context(nc.semaphore("sem_x"))
        sem_g = _es.enter_context(nc.semaphore("sem_g"))
        sem_t = _es.enter_context(nc.semaphore("sem_t"))
        sem_v = _es.enter_context(nc.semaphore("sem_v"))
        sem_s = _es.enter_context(nc.semaphore("sem_s"))
        sem_d = _es.enter_context(nc.semaphore("sem_d"))
        sem_r0 = _es.enter_context(nc.semaphore("sem_r0"))
        sem_r1 = _es.enter_context(nc.semaphore("sem_r1"))
        sem_r2 = _es.enter_context(nc.semaphore("sem_r2"))
        sem_q0 = _es.enter_context(nc.semaphore("sem_q0"))
        sem_q1 = _es.enter_context(nc.semaphore("sem_q1"))
        sem_w = _es.enter_context(nc.semaphore("sem_w"))
        sem_us = _es.enter_context(nc.semaphore("sem_us"))
        sem_qs0 = _es.enter_context(nc.semaphore("sem_qs0"))
        sem_qs1 = _es.enter_context(nc.semaphore("sem_qs1"))
        sem_zs = _es.enter_context(nc.semaphore("sem_zs"))
        sem_p = _es.enter_context(nc.semaphore("sem_p"))

        SEMS = {"x": sem_x, "g": sem_g, "t": sem_t, "v": sem_v, "s": sem_s, "d": sem_d,
                "r0": sem_r0, "r1": sem_r1, "r2": sem_r2, "q0": sem_q0, "q1": sem_q1,
                "w": sem_w, "us": sem_us, "qs0": sem_qs0, "qs1": sem_qs1, "zs": sem_zs,
                "p": sem_p}
        ops = {e: [] for e in ("sync", "gpsimd", "tensor", "vector", "scalar")}
        ctr = {k: 0 for k in SEMS}

        def emit(eng, fn):
            ops[eng].append(fn)

        def wait(eng, toks):
            for key, val in toks:
                if val > 0:
                    emit(eng, lambda e, s=SEMS[key], v=val: e.wait_ge(s, v))

        def bump(key, by=1):
            ctr[key] += by
            return ctr[key]

        def chv(t, ch, w=ST):
            return t[:, ch * w:(ch + 1) * w]

        xh3 = xh[:].rearrange("p (c w) -> p c w", w=260)
        XH_CUR = xh3[:, :, 1:ST + 1]
        XH_SHF = xh3[:, :, 0:ST]

        ring = [xbuf[0], xbuf[1], wbuf]
        bank_free = {0: ("v", 0), 1: ("v", 0), 2: ("v", 0), 3: ("v", 0), 4: ("v", 0), 5: ("v", 0)}
        stats_free = {0: ("v", 0), 1: ("v", 0)}
        rs_guard = [("v", 0)]
        prep_guard = [("t", 0)]
        rbuf_guard = [("v", 0)]
        vcp_guard = [("v", 0)]
        stg = [0]

        emit("vector", lambda e: e.memset(ones_l[:], 1.0 / 1024.0))
        emit("vector", lambda e: e.memset(ones_row[:], 1.0))
        emit("vector", lambda e: e.memset(epsA_row[:], ATT_EPS))
        emit("vector", lambda e: e.memset(xh[:], 0.0))
        emit("vector", lambda e: e.memset(s_app[:], 0.0))
        emit("vector", lambda e: e.memset(cst[:, 0:1], -0.5))
        emit("vector", lambda e: e.memset(cst[:, 1:2], -1.0))
        emit("vector", lambda e: e.memset(cst[:, 2:3], ATT_EPS))
        emit("vector", lambda e: e.memset(cst[:, 3:4], 0.70710678))
        emit("vector", lambda e: e.memset(cst[:, 4:5], LN_EPS))
        emit("vector", lambda e: e.memset(vpad[:], 1.0).then_inc(sem_v, 1))
        tok_init = bump("v")

        def gload(dst, src, key="w"):
            emit("gpsimd", lambda e, d=dst, s=src, sm=SEMS[key]: e.dma_start(out=d, in_=s).then_inc(sm, 16))
            return key, bump(key, 16)

        def sload(dst, src, key="x", eng="sync"):
            emit(eng, lambda e, d=dst, s=src, sm=SEMS[key]: e.dma_start(out=d, in_=s).then_inc(sm, 16))
            return key, bump(key, 16)

        tok_xld = [0] * (NST + 3)
        tok_xld[0] = sload(ring[0][:], x_st[0], "r0")
        tok_xld[1] = sload(ring[1][:], x_st[1], "r1")
        tok_xld[2] = sload(ring[2][:], x_st[2], "r2")

        gload(w_r[:], w_r_d)
        gload(w_k[:], w_k_d)
        gload(w_v[:], w_v_d)
        gload(wsp[:], w_qkv_d)
        tok_wr = tok_wk = tok_wv = tok_wqkv = ("w", 64)

        sload(mu_r_s[:], mu_r_d)
        sload(mu_k_s[:], mu_k_d)
        sload(bias_s[:], bias_d)
        tok_bmask = gload(bmask[:], bmask_d, "g")
        tok_small = ("x", ctr["x"])

        def ln_prep(src, deps):
            wait("scalar", deps + [prep_guard[0]])
            emit("scalar", lambda e: e.activation(xsq[:], src[:], ACT.Square).then_inc(sem_s, 1))
            tok_prep = bump("s")
            wait("vector", deps + [prep_guard[0]])
            emit("vector", lambda e: e.tensor_copy(xb16[:], src[:]).then_inc(sem_v, 1))
            tok_prepv = bump("v")
            return tok_prep, tok_prepv

        def ln_statsmm(preps):
            par = stg[0] % 2
            ps = ps_st[par]
            tok_prep, tok_prepv = preps
            wait("tensor", [("s", tok_prep), ("v", tok_prepv), stats_free[par], ("v", tok_init)])
            for c in range(NCH):
                emit("tensor", lambda e, c=c, ps=ps: e.matmul(
                    ps[:, 0:ST], ones_l[:], chv(xb16, c), start=(c == 0), stop=(c == NCH - 1)))
            for c in range(NCH):
                if c == NCH - 1:
                    emit("tensor", lambda e, c=c, ps=ps: e.matmul(
                        ps[:, ST:2 * ST], ones_l[:], chv(xsq, c),
                        start=False, stop=True, skip_group_check=True).then_inc(sem_t, 1))
                else:
                    emit("tensor", lambda e, c=c, ps=ps: e.matmul(
                        ps[:, ST:2 * ST], ones_l[:], chv(xsq, c),
                        start=False, stop=False, skip_group_check=True))
            tok_stats = bump("t")
            prep_guard[0] = ("t", tok_stats)
            wait("vector", [("t", tok_stats)])
            emit("vector", lambda e, ps=ps: e.tensor_copy(m_s[:], ps[:, 0:ST]))
            emit("vector", lambda e: e.tensor_mul(ts1[:], m_s[:], m_s[:]))
            emit("vector", lambda e, ps=ps: e.scalar_tensor_tensor(
                ts2[:], ts1[:], -1.0, ps[:, ST:2 * ST], ALU.mult, ALU.add).then_inc(sem_v, 1))
            tok_vareps = bump("v")
            stats_free[par] = ("v", tok_vareps)
            return tok_vareps

        def ln_stats(src, deps):
            return ln_statsmm(ln_prep(src, deps))

        def ln_norm(src, tok_vareps, out_xh, dstA=None):
            wait("scalar", [("v", tok_vareps), rs_guard[0]])
            emit("scalar", lambda e: e.activation(ts1[:], ts2[:], ACT.Ln, bias=cst[:, 4:5]))
            emit("scalar", lambda e: e.activation(
                rs_s[:], ts1[:], ACT.Exp, scale=cst[:, 0:1]).then_inc(sem_s, 1))
            tok_rs = bump("s")
            wait("vector", [("s", tok_rs)])
            rs_b = rs_s[:].unsqueeze(1).broadcast_to([128, NCH, ST])
            m_b = m_s[:].unsqueeze(1).broadcast_to([128, NCH, ST])
            src3 = src[:].rearrange("p (c w) -> p c w", w=ST)
            if out_xh:
                emit("vector", lambda e: e.tensor_tensor(XH_CUR, src3, m_b, ALU.subtract))
                emit("vector", lambda e: e.tensor_tensor(XH_CUR, XH_CUR, rs_b, ALU.mult).then_inc(sem_v, 1))
            else:
                u3 = ubuf[:].rearrange("p (c w) -> p c w", w=ST)
                d3 = dstA[:].rearrange("p (c w) -> p c w", w=ST)
                emit("vector", lambda e: e.tensor_tensor(u3, src3, m_b, ALU.subtract))
                emit("vector", lambda e: e.tensor_tensor(d3, u3, rs_b, ALU.mult).then_inc(sem_v, 1))
            tok_done = bump("v")
            rs_guard[0] = ("v", tok_done)
            return tok_done

        def mix_front(par, tok_xh):
            scr3 = scr[:].rearrange("p (c w) -> p c w", w=ST)
            wait("vector", [tok_small])
            emit("vector", lambda e: e.tensor_tensor(scr3, XH_CUR, XH_SHF, ALU.subtract))
            for c in range(NCH):
                emit("vector", lambda e, c=c, par=par: e.scalar_tensor_tensor(
                    chv(xrb[par], c), chv(scr, c), mu_r_s[:, c:c + 1], xh3[:, c, 0:ST],
                    ALU.mult, ALU.add))
            for c in range(NCH):
                if c == NCH - 1:
                    emit("vector", lambda e, c=c, par=par: e.scalar_tensor_tensor(
                        chv(xkb[par], c), chv(scr, c), mu_k_s[:, c:c + 1], xh3[:, c, 0:ST],
                        ALU.mult, ALU.add).then_inc(sem_v, 1))
                else:
                    emit("vector", lambda e, c=c, par=par: e.scalar_tensor_tensor(
                        chv(xkb[par], c), chv(scr, c), mu_k_s[:, c:c + 1], xh3[:, c, 0:ST],
                        ALU.mult, ALU.add))
            tok_mix = bump("v")
            emit("vector", lambda e: e.tensor_copy(xh3[:, :, 0:1], xh3[:, :, ST:ST + 1]))
            return tok_mix

        ps_pool = [ps_rot[0], ps_rot[1], ps_x, ps_s[0], ps_s[1], ps_s[2]]
        nbanks = [2]

        def unit_mms(dch, lhsT_of_c, rhs_of_c, wtoks):
            bk = dch % nbanks[0]
            wait("tensor", wtoks + [bank_free[bk]])
            for c in range(NCH):
                if c == NCH - 1:
                    emit("tensor", lambda e, c=c, bk=bk: e.matmul(
                        ps_pool[bk][:, 0:ST], lhsT_of_c(c), rhs_of_c(c),
                        start=(c == 0), stop=True).then_inc(sem_t, 1))
                else:
                    emit("tensor", lambda e, c=c, bk=bk: e.matmul(
                        ps_pool[bk][:, 0:ST], lhsT_of_c(c), rhs_of_c(c),
                        start=(c == 0), stop=False))
            return bk, bump("t")

        def ffn_r(par, tok_mix, interleave=None):
            inter = list(interleave) if interleave else []
            for d in range(NCH):
                if inter:
                    inter.pop(0)()
                wt = [("v", tok_mix), tok_wr] if d == 0 else []
                bk, tok = unit_mms(
                    d, lambda c, d=d: w_r[:, c * 1024 + d * 128:c * 1024 + (d + 1) * 128],
                    lambda c, par=par: chv(xrb[par], c), wt)
                wait("scalar", [("t", tok)] + ([rbuf_guard[0]] if d == 0 else []))
                emit("scalar", lambda e, d=d, bk=bk: e.activation(
                    chv(rbuf, d), ps_pool[bk][:, 0:ST], ACT.Sigmoid).then_inc(sem_s, 1))
                bank_free[bk] = ("s", bump("s"))

        def ffn_k(par):
            tok_sq_last = 0
            for d in range(NCH):
                wt = [tok_wk] if d == 0 else []
                bk, tok = unit_mms(
                    d, lambda c, d=d: w_k[:, c * 1024 + d * 128:c * 1024 + (d + 1) * 128],
                    lambda c, par=par: chv(xkb[par], c), wt)
                wait("scalar", [("t", tok)])
                emit("scalar", lambda e, d=d, bk=bk: e.activation(
                    chv(vcp, d), ps_pool[bk][:, 0:ST], ACT.Relu).then_inc(sem_s, 1))
                bank_free[bk] = ("s", bump("s"))
                emit("scalar", lambda e, d=d: e.activation(
                    chv(kact, d), chv(vcp, d), ACT.Square).then_inc(sem_s, 1))
                tok_sq_last = bump("s")
            return tok_sq_last

        def ffn_v(tok_sq_last, resid, out_buf, resid_dep, out_guard, tt_gp=None):
            cp_toks = []
            for d in range(NCH):
                wt = [("s", tok_sq_last), tok_wv] if d == 0 else []
                bk, tok = unit_mms(
                    d, lambda c, d=d: w_v[:, c * 1024 + d * 128:c * 1024 + (d + 1) * 128],
                    lambda c: chv(kact, c), wt)
                wait("scalar", [("t", tok)])
                emit("scalar", lambda e, d=d, bk=bk: e.copy(
                    chv(vcp, d), ps_pool[bk][:, 0:ST]).then_inc(sem_s, 1))
                cp_toks.append(bump("s"))
                bank_free[bk] = ("s", cp_toks[-1])
            rsrc = out_buf if resid is None else resid
            if tt_gp is not None:
                tts = []
                wait("gpsimd", [tt_gp])
                for d in range(NCH):
                    wait("gpsimd", [("s", cp_toks[d])])
                    emit("gpsimd", lambda e, d=d: e.tensor_tensor(
                        chv(scr, d), chv(rbuf, d), chv(vcp, d), ALU.mult).then_inc(sem_p, 1))
                    tts.append(bump("p"))
                wait("vector", resid_dep + [out_guard])
                for d in range(NCH):
                    wait("vector", [("p", tts[d])])
                    if d == NCH - 1:
                        emit("vector", lambda e, d=d: e.scalar_tensor_tensor(
                            chv(out_buf, d), chv(scr, d), 0.5, chv(rsrc, d),
                            ALU.mult, ALU.add).then_inc(sem_v, 1))
                    else:
                        emit("vector", lambda e, d=d: e.scalar_tensor_tensor(
                            chv(out_buf, d), chv(scr, d), 0.5, chv(rsrc, d),
                            ALU.mult, ALU.add))
                tok_out = bump("v")
                rbuf_guard[0] = ("p", tts[-1])
                vcp_guard[0] = ("p", tts[-1])
                return tok_out
            wait("vector", [("s", cp_toks[-1])] + resid_dep + [out_guard])
            for d in range(NCH):
                emit("vector", lambda e, d=d: e.tensor_mul(chv(scr, d), chv(rbuf, d), chv(vcp, d)))
                if d == NCH - 1:
                    emit("vector", lambda e, d=d, rsrc=rsrc: e.scalar_tensor_tensor(
                        chv(out_buf, d), chv(scr, d), 0.5, chv(rsrc, d),
                        ALU.mult, ALU.add).then_inc(sem_v, 1))
                else:
                    emit("vector", lambda e, d=d, rsrc=rsrc: e.scalar_tensor_tensor(
                        chv(out_buf, d), chv(scr, d), 0.5, chv(rsrc, d),
                        ALU.mult, ALU.add))
            tok_out = bump("v")
            rbuf_guard[0] = ("v", tok_out)
            vcp_guard[0] = ("v", tok_out)
            return tok_out

        tok_mix_a1 = [0] * NST
        tok_u_all = [0] * NST
        tok_uspill = [0] * NST

        nbanks[0] = 6
        preps = ln_prep(ring[0], [tok_xld[0]])
        for j in range(NST + 1):
            if j < NST:
                tv = ln_statsmm(preps)
            if j >= 1:
                ffn_r((j - 1) % 2, tok_mix_a1[j - 1])
            if j < NST:
                tok_xh = ln_norm(ring[j % 3], tv, True)
                tok_mix_a1[j] = mix_front(j % 2, tok_xh)
                stg[0] += 1
            if j >= 1:
                tok_sq = ffn_k((j - 1) % 2)
            if j + 1 < NST:
                preps = ln_prep(ring[(j + 1) % 3], [tok_xld[j + 1]])
            if j >= 1:
                i = j - 1
                guard = ("us", tok_uspill[i - 1]) if i >= 1 else ("us", 0)
                tok_u_all[i] = ffn_v(tok_sq, ring[i % 3], ubuf, [], guard)
                wait("gpsimd", [("v", tok_u_all[i])])
                emit("gpsimd", lambda e, i=i: e.dma_start(out=u_scr[i], in_=ubuf[:]).then_inc(sem_us, 16))
                tok_uspill[i] = bump("us", 16)
                if i + 3 < NST:
                    wait("sync", [("v", tok_u_all[i])])
                    tok_xld[i + 3] = sload(ring[i % 3][:], x_st[i + 3], "r" + str(i % 3))

        nbanks[0] = 3
        s_bank_guards = [bank_free[3], bank_free[4], bank_free[5]]
        tok_uld = [0] * (NST + 3)
        wait("sync", [("us", 16 * NST), ("v", tok_u_all[NST - 1])])
        tok_uld[0] = sload(ring[0][:], u_scr[0], "r0")
        tok_uld[1] = sload(ring[1][:], u_scr[1], "r1")
        tok_uld[2] = sload(ring[2][:], u_scr[2], "r2")

        tok_x2 = [0] * NST
        tok_qspill = [0] * NST
        tok_smm = [0] * (2 * NST)
        tok_kv_rhs_last = 0

        for j in range(NST + 1):
            if j >= 1:
                i = j - 1
                par = i % 2
                tok_q_last = 0
                for d in range(NCH):
                    wt = [("v", tok_x2[i]), tok_wqkv] if d == 0 else []
                    bk, tok = unit_mms(
                        d, lambda c, d=d: wsp[:, c * 3072 + d * 128:c * 3072 + (d + 1) * 128],
                        lambda c, par=par: chv(bfA[par], c), wt)
                    wait("scalar", [("t", tok)] + ([tok_qspill[i - 2]] if (d == 0 and i >= 2) else []))
                    emit("scalar", lambda e, d=d, bk=bk, par=par: e.activation(
                        chv(qbf[par], d), ps_pool[bk][:, 0:ST], ACT.Relu).then_inc(sem_s, 1))
                    tok_q_last = bump("s")
                    bank_free[bk] = ("s", tok_q_last)
                wait("gpsimd", [("s", tok_q_last)])
                emit("gpsimd", lambda e, i=i, par=par, sm=(sem_qs0 if i % 2 == 0 else sem_qs1): e.dma_start(
                    out=q_scr[i], in_=qbf[par][:]).then_inc(sm, 16))
                tok_qspill[i] = ("qs" + str(i % 2), bump("qs" + str(i % 2), 16))
            if j < NST:
                if j == 0:
                    wait("vector", [("us", 16 * NST)])
                    preps2 = ln_prep(ring[0], [tok_uld[0]])
                tv = ln_statsmm(preps2)
                tok_x2[j] = ln_norm(ring[j % 3], tv, False, dstA=bfA[j % 2])
                stg[0] += 1
            if j >= 1:
                i = j - 1
                par = i % 2
                for h in range(2):
                    gi = 2 * i + h
                    tok_kv_last = 0
                    for n in range(4):
                        bk = n % 2
                        wait("tensor", [bank_free[bk]])
                        for c in range(NCH):
                            lh = bfA[par][:, c * ST + h * 128:c * ST + (h + 1) * 128]
                            rh = wsp[:, c * 3072 + 1024 + n * 512:c * 3072 + 1024 + (n + 1) * 512]
                            if c == NCH - 1:
                                emit("tensor", lambda e, lh=lh, rh=rh, bk=bk: e.matmul(
                                    ps_rot[bk][:], lh, rh, start=False, stop=True).then_inc(sem_t, 1))
                            elif c == 0:
                                emit("tensor", lambda e, lh=lh, rh=rh, bk=bk: e.matmul(
                                    ps_rot[bk][:], lh, rh, start=True, stop=False))
                            else:
                                emit("tensor", lambda e, lh=lh, rh=rh, bk=bk: e.matmul(
                                    ps_rot[bk][:], lh, rh, start=False, stop=False))
                        tok_mm = bump("t")
                        if n == 3:
                            tok_kv_rhs_last = tok_mm
                        guard = [("t", tok_smm[gi - 1])] if (n == 0 and gi >= 1) else []
                        wait("scalar", [("t", tok_mm)] + guard)
                        if n < 2:
                            emit("scalar", lambda e, n=n, bk=bk: e.activation(
                                ktm[:, n * 512:(n + 1) * 512], ps_rot[bk][:], ACT.Relu).then_inc(sem_s, 1))
                        else:
                            dstv = vpad[:].rearrange("p (hh e) -> p hh e", e=33)[:, (n - 2) * 16:(n - 1) * 16, 0:32]
                            emit("scalar", lambda e, bk=bk, dstv=dstv: e.activation(
                                dstv, ps_rot[bk][:].rearrange("p (hh e) -> p hh e", e=32),
                                ACT.Relu).then_inc(sem_s, 1))
                        tok_kv_last = bump("s")
                        bank_free[bk] = ("s", tok_kv_last)
                    wait("tensor", [("s", tok_kv_last)] + (s_bank_guards if gi == 0 else []))
                    for g in range(8):
                        pst = ps_s[g // 3]
                        col = (g % 3) * 132
                        start = (gi == 0 and g % 3 == 0)
                        stop = (gi == 2 * NST - 1)
                        if g == 7:
                            emit("tensor", lambda e, g=g, pst=pst, col=col, start=start, stop=stop: e.matmul(
                                pst[:, col:col + 132], ktm[:, g * 128:(g + 1) * 128],
                                vpad[:, g * 132:(g + 1) * 132], start=start, stop=stop,
                                skip_group_check=True).then_inc(sem_t, 1))
                        else:
                            emit("tensor", lambda e, g=g, pst=pst, col=col, start=start, stop=stop: e.matmul(
                                pst[:, col:col + 132], ktm[:, g * 128:(g + 1) * 128],
                                vpad[:, g * 132:(g + 1) * 132], start=start, stop=stop,
                                skip_group_check=True))
                    tok_smm[gi] = bump("t")

                if i + 3 < NST:
                    wait("sync", [("v", tok_x2[i])])
                    tok_uld[i + 3] = sload(ring[i % 3][:], u_scr[i + 3], "r" + str(i % 3))
            if j < NST and j + 1 < NST:
                preps2 = ln_prep(ring[(j + 1) % 3], [tok_uld[j + 1]])

        wait("vector", [("t", tok_smm[2 * NST - 1]), tok_bmask])
        for g in range(8):
            pst = ps_s[g // 3]
            col = (g % 3) * 132
            for jj in range(4):
                emit("vector", lambda e, g=g, jj=jj, pst=pst, col=col: e.tensor_copy(
                    s_app[32 * jj:32 * (jj + 1), g * 128 + 32 * jj:g * 128 + 32 * (jj + 1)],
                    pst[32 * jj:32 * (jj + 1), col + 33 * jj:col + 33 * jj + 32]))
                emit("vector", lambda e, g=g, jj=jj, pst=pst, col=col: e.tensor_copy(
                    dv[32 * jj:32 * (jj + 1), g:g + 1],
                    pst[32 * jj:32 * (jj + 1), col + 33 * jj + 32:col + 33 * jj + 33]))
        for g in range(8):
            if g == 7:
                emit("vector", lambda e, g=g: e.tensor_scalar(
                    denrep[:, g * 128:(g + 1) * 128], bmask[:], dv[:, g:g + 1],
                    None, ALU.mult).then_inc(sem_v, 1))
            else:
                emit("vector", lambda e, g=g: e.tensor_scalar(
                    denrep[:, g * 128:(g + 1) * 128], bmask[:], dv[:, g:g + 1],
                    None, ALU.mult))
        tok_sevac = bump("v")

        wait("gpsimd", [("t", tok_kv_rhs_last)])
        gload(wsp[:, 0:NCH * 1024], w_p_d)
        tok_wp = ("w", 80)

        tok_uld2 = [0] * (NST + 2)
        tok_qld = [0] * (NST + 2)
        wait("sync", [("v", tok_x2[NST - 1])])
        tok_uld2[0] = sload(xbuf[0][:], u_scr[0], "r0")
        wait("sync", [("v", tok_x2[NST - 3])])
        tok_uld2[1] = sload(xbuf[1][:], u_scr[1], "r1")
        tok_wbuf_free = ("v", tok_x2[NST - 2])
        wait("sync", [("qs0", ctr["qs0"]), ("qs1", ctr["qs1"])])
        tok_qld[0] = sload(qbf[0][:], q_scr[0], "q0")
        tok_qld[1] = sload(qbf[1][:], q_scr[1], "q1")

        emit("vector", lambda e: e.memset(xh3[:, :, 0:1], 0.0))

        app_bank_free = None
        tok_div = [[0] * 8 for _ in range(NST)]
        tok_mix_b = [0] * NST
        tok_w_all = [0] * NST
        tok_zst = [0] * NST
        tok_zprev = [0]

        tok_app = [[0] * 8 for _ in range(NST)]

        def emit_app_mm(i, g):
            par = i % 2
            bk = g % 3
            pst = ps_s[bk]
            if g >= 3:
                need = 8 * i + (g - 3) + 1
            elif i >= 1:
                need = 8 * (i - 1) + {0: 6, 1: 7, 2: 5}[bk] + 1
            else:
                need = 0
            wt = [("d", need)]
            if g == 0:
                wt += [tok_qld[i]]
            if i == 0:
                wt += [("v", tok_sevac)]
            wait("tensor", wt)
            emit("tensor", lambda e, g=g, pst=pst, par=par: e.matmul(
                pst[:, 0:ST], s_app[:, g * 128:(g + 1) * 128], chv(qbf[par], g),
                start=True, stop=True, skip_group_check=True))
            emit("tensor", lambda e, g=g, pst=pst, par=par: e.matmul(
                pst[:, ST:2 * ST], denrep[:, g * 128:(g + 1) * 128], chv(qbf[par], g),
                start=False, stop=False, skip_group_check=True))
            emit("tensor", lambda e, pst=pst: e.matmul(
                pst[:, ST:2 * ST], ones_row[:], epsA_row[:],
                start=False, stop=True, skip_group_check=True).then_inc(sem_t, 1))
            tok_app[i][g] = bump("t")

        def den_div(i):
            par = i % 2
            for g in range(8):
                pst = ps_s[g % 3]
                half = (g % 2) * ST
                wt = [("t", tok_app[i][g])]
                if g >= 2:
                    wt.append(("d", 8 * i + g - 1))
                wait("scalar", wt)
                emit("scalar", lambda e, pst=pst: e.activation(
                    ts2[:], pst[:, ST:2 * ST], ACT.Ln))
                emit("scalar", lambda e, half=half: e.activation(
                    drec[:, half:half + ST], ts2[:], ACT.Exp,
                    scale=cst[:, 1:2]).then_inc(sem_s, 1))
                tok_dr = bump("s")
                wait("vector", [("s", tok_dr)])
                emit("vector", lambda e, g=g, pst=pst, par=par, half=half: e.tensor_mul(
                    chv(bfA[par], g), pst[:, 0:ST], drec[:, half:half + ST]).then_inc(sem_d, 1))
                bump("d")
                tok_div[i][g] = ("d", 8 * i + g + 1)

        for g in range(8):
            emit_app_mm(0, g)
        den_div(0)

        pp = [wbuf, ubuf]
        for j in range(NST + 1):
            i = j - 1
            if j < NST:
                par = j % 2
                tok_wev = 0
                for d in range(NCH):
                    wt = [tok_div[j][7], tok_wp] if d == 0 else []
                    bk, tok = unit_mms(
                        d, lambda c, d=d: wsp[:, c * 1024 + d * 128:c * 1024 + (d + 1) * 128],
                        lambda c, par=par: chv(bfA[par], c), wt)
                    wextra = []
                    if d == 0:
                        wextra = [tok_uld2[j]]
                        if j == 0:
                            wextra.append(tok_wbuf_free)
                        elif j == 1:
                            wextra.append(("v", tok_x2[NST - 1]))
                        else:
                            wextra.append(("zs", tok_zst[j - 2]))
                    wait("vector", [("t", tok)] + wextra)
                    emit("vector", lambda e, d=d, bk=bk, par=par, j=j: e.scalar_tensor_tensor(
                        chv(pp[j % 2], d), ps_pool[bk][:, 0:ST], bias_s[:, d:d + 1],
                        chv(xbuf[par], d), ALU.add, ALU.add).then_inc(sem_v, 1))
                    tok_wev = bump("v")
                    bank_free[bk] = ("v", tok_wev)
                tok_w_all[j] = tok_wev
            if j < NST:
                preps3 = ln_prep(pp[j % 2], [("v", tok_w_all[j])])
            if j >= 1:
                ffn_r(i % 2, tok_mix_b[i])
            if j < NST:
                tvb = ln_statsmm(preps3)
                tok_xh3 = ln_norm(pp[j % 2], tvb, True)
            if j >= 1:
                tok_sq = ffn_k(i % 2)
            if j < NST and j + 1 < NST:
                for g in range(8):
                    emit_app_mm(j + 1, g)
                den_div(j + 1)
            if j >= 1:
                tok_z = ffn_v(tok_sq, None, pp[i % 2], [], ("v", 0),
                              tt_gp=("v", tok_mix_b[i]))
                tok_zprev[0] = tok_z
                wait("gpsimd", [("v", tok_z)])
                emit("gpsimd", lambda e, i=i: e.dma_start(
                    out=z_out[i], in_=pp[i % 2][:]).then_inc(sem_zs, 16))
                tok_zst[i] = bump("zs", 16)
            if j < NST:
                tok_mix_b[j] = mix_front(j % 2, tok_xh3)
                stg[0] += 1
            if j < NST:
                if j + 2 < NST:
                    wait("sync", [("v", tok_w_all[j])])
                    tok_uld2[j + 2] = sload(xbuf[j % 2][:], u_scr[j + 2], "r" + str(j % 2))
                    wait("sync", [tok_div[j][7]])
                    tok_qld[j + 2] = sload(qbf[j % 2][:], q_scr[j + 2], "q" + str(j % 2))

        wait("gpsimd", [(k, ctr[k]) for k in ("g", "w", "us", "qs0", "qs1", "zs")])
        wait("sync", [(k, ctr[k]) for k in ("x", "r0", "r1", "r2", "q0", "q1")])

        with nc.Block() as block:
            @block.sync
            def _(sync):
                for f in ops["sync"]:
                    f(sync)

            @block.gpsimd
            def _(gpsimd):
                for f in ops["gpsimd"]:
                    f(gpsimd)

            @block.tensor
            def _(tensor):
                for f in ops["tensor"]:
                    f(tensor)

            @block.vector
            def _(vector):
                for f in ops["vector"]:
                    f(vector)

            @block.scalar
            def _(scalar):
                for f in ops["scalar"]:
                    f(scalar)

    return nc


def _stage_fm(mat_t):
    Dd = mat_t.shape[1]
    return np.ascontiguousarray(
        mat_t.reshape(NCH, 128, Dd).transpose(1, 0, 2).reshape(128, NCH * Dd))


def kernel(L0_attn_x, L0_attn_qkv_weight, L0_attn_proj_weight, L0_attn_proj_bias,
           L0_ffn_W_r, L0_ffn_W_k, L0_ffn_W_v, L0_ffn_mu_r, L0_ffn_mu_k):
    x = np.ascontiguousarray(np.asarray(L0_attn_x, dtype=np.float32))
    w_qkv = _stage_fm(np.ascontiguousarray(np.asarray(L0_attn_qkv_weight, np.float32).T))
    w_p = _stage_fm(np.ascontiguousarray(np.asarray(L0_attn_proj_weight, np.float32).T))
    w_r = _stage_fm(np.ascontiguousarray(np.asarray(L0_ffn_W_r, np.float32).T))
    w_k = _stage_fm(np.ascontiguousarray(np.asarray(L0_ffn_W_k, np.float32).T))
    w_v = _stage_fm(np.ascontiguousarray(np.asarray(L0_ffn_W_v, np.float32).T))
    mu_r = np.ascontiguousarray(np.asarray(L0_ffn_mu_r, np.float32).reshape(NCH, 128).T)
    mu_k = np.ascontiguousarray(np.asarray(L0_ffn_mu_k, np.float32).reshape(NCH, 128).T)
    bias = np.ascontiguousarray(np.asarray(L0_attn_proj_bias, np.float32).reshape(NCH, 128).T)
    bmask = np.kron(np.eye(4, dtype=np.float32), np.ones((32, 32), np.float32))

    nc = build()
    in_maps = []
    for b in range(B):
        xs = np.ascontiguousarray(
            x[b].T.reshape(NCH, 128, NST, ST).transpose(2, 1, 0, 3).reshape(NST, 128, NCH * ST))
        in_maps.append({
            "x_st": xs, "w_r_d": w_r, "w_k_d": w_k, "w_v_d": w_v,
            "w_qkv_d": w_qkv, "w_p_d": w_p, "mu_r_d": mu_r, "mu_k_d": mu_k,
            "bias_d": bias, "bmask_d": bmask,
        })
    res = run_bass_kernel_spmd(nc, in_maps, core_ids=list(range(B)))
    out = np.empty((B, N, C), dtype=np.float32)
    for b in range(B):
        z = np.asarray(res.results[b]["z_out"])
        out[b] = z.reshape(NST, 128, NCH, ST).transpose(0, 3, 2, 1).reshape(N, C)
    return out


# version 39
# speedup vs baseline: 1.0104x; 1.0104x over previous
import numpy as np
import concourse.bass as bass
import concourse.mybir as mybir
from concourse.bass_utils import run_bass_kernel_spmd

F32 = mybir.dt.float32
BF16 = mybir.dt.bfloat16
ALU = mybir.AluOpType
ACT = mybir.ActivationFunctionType

B, N, C = 8, 4096, 1024
ST = 256
NST = N // ST
NCH = C // 128
LN_EPS = 1e-5
ATT_EPS = 1e-15


def build(debug=False):
    nc = bass.Bass("TRN2", target_bir_lowering=False, detect_race_conditions=False)

    x_st = nc.dram_tensor("x_st", [NST, 128, NCH * ST], F32, kind="ExternalInput").ap()
    w_r_d = nc.dram_tensor("w_r_d", [128, NCH * 1024], F32, kind="ExternalInput").ap()
    w_k_d = nc.dram_tensor("w_k_d", [128, NCH * 1024], F32, kind="ExternalInput").ap()
    w_v_d = nc.dram_tensor("w_v_d", [128, NCH * 1024], F32, kind="ExternalInput").ap()
    w_qkv_d = nc.dram_tensor("w_qkv_d", [128, NCH * 3072], F32, kind="ExternalInput").ap()
    w_p_d = nc.dram_tensor("w_p_d", [128, NCH * 1024], F32, kind="ExternalInput").ap()
    mu_r_d = nc.dram_tensor("mu_r_d", [128, NCH], F32, kind="ExternalInput").ap()
    mu_k_d = nc.dram_tensor("mu_k_d", [128, NCH], F32, kind="ExternalInput").ap()
    bias_d = nc.dram_tensor("bias_d", [128, NCH], F32, kind="ExternalInput").ap()
    bmask_d = nc.dram_tensor("bmask_d", [128, 128], F32, kind="ExternalInput").ap()

    sk = "ExternalOutput" if debug else "Internal"
    u_scr = nc.dram_tensor("u_scr", [NST, 128, NCH * ST], F32, kind=sk).ap()
    q_scr = nc.dram_tensor("q_scr", [NST, 128, NCH * ST], BF16, kind=sk).ap()
    z_out = nc.dram_tensor("z_out", [NST, 128, NCH * ST], F32, kind="ExternalOutput").ap()

    from contextlib import ExitStack
    with ExitStack() as _es:
        def SB(name, shape, dt):
            return _es.enter_context(nc.sbuf_tensor(name, shape, dt))

        def PSUM(name, shape):
            return _es.enter_context(nc.psum_tensor(name, shape, F32))

        w_r = SB("w_r", [128, NCH * 1024], BF16)
        w_k = SB("w_k", [128, NCH * 1024], BF16)
        w_v = SB("w_v", [128, NCH * 1024], BF16)
        wsp = SB("wsp", [128, NCH * 3072], BF16)
        xbuf = [SB("xbuf0", [128, NCH * ST], F32), SB("xbuf1", [128, NCH * ST], F32)]
        ubuf = SB("ubuf", [128, NCH * ST], F32)
        wbuf = SB("wbuf", [128, NCH * ST], F32)
        xh = SB("xh", [128, NCH * 260], F32)
        scr = SB("scr", [128, NCH * ST], BF16)
        rbuf = SB("rbuf", [128, NCH * ST], BF16)
        vcp = SB("vcp", [128, NCH * ST], BF16)
        xrb = [SB("xr0", [128, NCH * ST], BF16), SB("xr1", [128, NCH * ST], BF16)]
        xkb = [SB("xk0", [128, NCH * ST], BF16), SB("xk1", [128, NCH * ST], BF16)]
        xb16 = SB("xb16", [128, NCH * ST], BF16)
        xsq = SB("xsq", [128, NCH * ST], BF16)
        kact = SB("kact", [128, NCH * ST], BF16)
        bfA = [SB("bfA0", [128, NCH * ST], BF16), SB("bfA1", [128, NCH * ST], BF16)]
        qbf = [SB("qb0", [128, NCH * ST], BF16), SB("qb1", [128, NCH * ST], BF16)]
        ktm = SB("ktm", [128, 1024], BF16)
        vpad = SB("vpad", [128, 1056], BF16)
        mu_r_s = SB("mu_r_s", [128, NCH], F32)
        mu_k_s = SB("mu_k_s", [128, NCH], F32)
        bias_s = SB("bias_s", [128, NCH], F32)
        bmask = SB("bmask", [128, 128], BF16)
        ones_l = SB("ones_l", [128, 128], BF16)
        ones_row = SB("ones_row", [1, 128], BF16)
        epsA_row = SB("epsA_row", [1, ST], BF16)
        cst = SB("cst", [128, 5], F32)
        ts1 = SB("ts1", [128, ST], F32)
        ts2 = SB("ts2", [128, ST], F32)
        drec = SB("drec", [128, 2 * ST], F32)
        rs_s = SB("rs_s", [128, ST], F32)
        m_s = SB("m_s", [128, ST], F32)
        s_app = SB("s_app", [128, 8 * 128], BF16)
        denrep = SB("denrep", [128, 8 * 128], BF16)
        dv = SB("dv", [128, 8], F32)

        ps_st = [PSUM("ps_st0", [128, 512]), PSUM("ps_st1", [128, 512])]
        ps_rot = [PSUM("ps_rot0", [128, 512]), PSUM("ps_rot1", [128, 512])]
        ps_s = [PSUM("ps_s0", [128, 512]), PSUM("ps_s1", [128, 512]), PSUM("ps_s2", [128, 512])]
        ps_x = PSUM("ps_x", [128, 512])

        sem_x = _es.enter_context(nc.semaphore("sem_x"))
        sem_g = _es.enter_context(nc.semaphore("sem_g"))
        sem_t = _es.enter_context(nc.semaphore("sem_t"))
        sem_v = _es.enter_context(nc.semaphore("sem_v"))
        sem_s = _es.enter_context(nc.semaphore("sem_s"))
        sem_d = _es.enter_context(nc.semaphore("sem_d"))
        sem_r0 = _es.enter_context(nc.semaphore("sem_r0"))
        sem_r1 = _es.enter_context(nc.semaphore("sem_r1"))
        sem_r2 = _es.enter_context(nc.semaphore("sem_r2"))
        sem_q0 = _es.enter_context(nc.semaphore("sem_q0"))
        sem_q1 = _es.enter_context(nc.semaphore("sem_q1"))
        sem_w = _es.enter_context(nc.semaphore("sem_w"))
        sem_us = _es.enter_context(nc.semaphore("sem_us"))
        sem_qs0 = _es.enter_context(nc.semaphore("sem_qs0"))
        sem_qs1 = _es.enter_context(nc.semaphore("sem_qs1"))
        sem_zs = _es.enter_context(nc.semaphore("sem_zs"))
        sem_p = _es.enter_context(nc.semaphore("sem_p"))

        SEMS = {"x": sem_x, "g": sem_g, "t": sem_t, "v": sem_v, "s": sem_s, "d": sem_d,
                "r0": sem_r0, "r1": sem_r1, "r2": sem_r2, "q0": sem_q0, "q1": sem_q1,
                "w": sem_w, "us": sem_us, "qs0": sem_qs0, "qs1": sem_qs1, "zs": sem_zs,
                "p": sem_p}
        ops = {e: [] for e in ("sync", "gpsimd", "tensor", "vector", "scalar")}
        ctr = {k: 0 for k in SEMS}

        def emit(eng, fn):
            ops[eng].append(fn)

        def wait(eng, toks):
            for key, val in toks:
                if val > 0:
                    emit(eng, lambda e, s=SEMS[key], v=val: e.wait_ge(s, v))

        def bump(key, by=1):
            ctr[key] += by
            return ctr[key]

        def chv(t, ch, w=ST):
            return t[:, ch * w:(ch + 1) * w]

        xh3 = xh[:].rearrange("p (c w) -> p c w", w=260)
        XH_CUR = xh3[:, :, 1:ST + 1]
        XH_SHF = xh3[:, :, 0:ST]

        ring = [xbuf[0], xbuf[1], wbuf]
        bank_free = {0: ("v", 0), 1: ("v", 0), 2: ("v", 0), 3: ("v", 0), 4: ("v", 0), 5: ("v", 0)}
        stats_free = {0: ("v", 0), 1: ("v", 0)}
        rs_guard = [("v", 0)]
        prep_guard = [("t", 0)]
        rbuf_guard = [("v", 0)]
        vcp_guard = [("v", 0)]
        stg = [0]

        emit("vector", lambda e: e.memset(ones_l[:], 1.0 / 1024.0))
        emit("vector", lambda e: e.memset(ones_row[:], 1.0))
        emit("vector", lambda e: e.memset(epsA_row[:], ATT_EPS))
        emit("vector", lambda e: e.memset(xh[:], 0.0))
        emit("vector", lambda e: e.memset(s_app[:], 0.0))
        emit("vector", lambda e: e.memset(cst[:, 0:1], -0.5))
        emit("vector", lambda e: e.memset(cst[:, 1:2], -1.0))
        emit("vector", lambda e: e.memset(cst[:, 2:3], ATT_EPS))
        emit("vector", lambda e: e.memset(cst[:, 3:4], 0.70710678))
        emit("vector", lambda e: e.memset(cst[:, 4:5], LN_EPS))
        emit("vector", lambda e: e.memset(vpad[:], 1.0).then_inc(sem_v, 1))
        tok_init = bump("v")

        def gload(dst, src, key="w"):
            emit("gpsimd", lambda e, d=dst, s=src, sm=SEMS[key]: e.dma_start(out=d, in_=s).then_inc(sm, 16))
            return key, bump(key, 16)

        def sload(dst, src, key="x"):
            emit("sync", lambda e, d=dst, s=src, sm=SEMS[key]: e.dma_start(out=d, in_=s).then_inc(sm, 16))
            return key, bump(key, 16)

        gload(w_r[:], w_r_d)
        gload(w_k[:], w_k_d)
        gload(w_v[:], w_v_d)
        gload(wsp[:], w_qkv_d)
        tok_wr = tok_wk = tok_wv = tok_wqkv = ("w", 64)

        sload(mu_r_s[:], mu_r_d)
        sload(mu_k_s[:], mu_k_d)
        sload(bias_s[:], bias_d)
        tok_bmask = gload(bmask[:], bmask_d, "g")
        tok_small = ("x", ctr["x"])

        def ln_prep(src, deps):
            wait("scalar", deps + [prep_guard[0]])
            emit("scalar", lambda e: e.activation(xsq[:], src[:], ACT.Square).then_inc(sem_s, 1))
            tok_prep = bump("s")
            wait("vector", deps + [prep_guard[0]])
            emit("vector", lambda e: e.tensor_copy(xb16[:], src[:]).then_inc(sem_v, 1))
            tok_prepv = bump("v")
            return tok_prep, tok_prepv

        def ln_statsmm(preps):
            par = stg[0] % 2
            ps = ps_st[par]
            tok_prep, tok_prepv = preps
            wait("tensor", [("s", tok_prep), ("v", tok_prepv), stats_free[par], ("v", tok_init)])
            for c in range(NCH):
                emit("tensor", lambda e, c=c, ps=ps: e.matmul(
                    ps[:, 0:ST], ones_l[:], chv(xb16, c), start=(c == 0), stop=(c == NCH - 1)))
            for c in range(NCH):
                if c == NCH - 1:
                    emit("tensor", lambda e, c=c, ps=ps: e.matmul(
                        ps[:, ST:2 * ST], ones_l[:], chv(xsq, c),
                        start=False, stop=True, skip_group_check=True).then_inc(sem_t, 1))
                else:
                    emit("tensor", lambda e, c=c, ps=ps: e.matmul(
                        ps[:, ST:2 * ST], ones_l[:], chv(xsq, c),
                        start=False, stop=False, skip_group_check=True))
            tok_stats = bump("t")
            prep_guard[0] = ("t", tok_stats)
            wait("vector", [("t", tok_stats)])
            emit("vector", lambda e, ps=ps: e.tensor_copy(m_s[:], ps[:, 0:ST]))
            emit("vector", lambda e: e.tensor_mul(ts1[:], m_s[:], m_s[:]))
            emit("vector", lambda e, ps=ps: e.scalar_tensor_tensor(
                ts2[:], ts1[:], -1.0, ps[:, ST:2 * ST], ALU.mult, ALU.add).then_inc(sem_v, 1))
            tok_vareps = bump("v")
            stats_free[par] = ("v", tok_vareps)
            return tok_vareps

        def ln_stats(src, deps):
            return ln_statsmm(ln_prep(src, deps))

        def ln_norm(src, tok_vareps, out_xh, dstA=None):
            wait("scalar", [("v", tok_vareps), rs_guard[0]])
            emit("scalar", lambda e: e.activation(ts1[:], ts2[:], ACT.Ln, bias=cst[:, 4:5]))
            emit("scalar", lambda e: e.activation(
                rs_s[:], ts1[:], ACT.Exp, scale=cst[:, 0:1]).then_inc(sem_s, 1))
            tok_rs = bump("s")
            wait("vector", [("s", tok_rs)])
            rs_b = rs_s[:].unsqueeze(1).broadcast_to([128, NCH, ST])
            m_b = m_s[:].unsqueeze(1).broadcast_to([128, NCH, ST])
            src3 = src[:].rearrange("p (c w) -> p c w", w=ST)
            if out_xh:
                emit("vector", lambda e: e.tensor_tensor(XH_CUR, src3, m_b, ALU.subtract))
                emit("vector", lambda e: e.tensor_tensor(XH_CUR, XH_CUR, rs_b, ALU.mult).then_inc(sem_v, 1))
            else:
                u3 = ubuf[:].rearrange("p (c w) -> p c w", w=ST)
                d3 = dstA[:].rearrange("p (c w) -> p c w", w=ST)
                emit("vector", lambda e: e.tensor_tensor(u3, src3, m_b, ALU.subtract))
                emit("vector", lambda e: e.tensor_tensor(d3, u3, rs_b, ALU.mult).then_inc(sem_v, 1))
            tok_done = bump("v")
            rs_guard[0] = ("v", tok_done)
            return tok_done

        def mix_front(par, tok_xh):
            scr3 = scr[:].rearrange("p (c w) -> p c w", w=ST)
            wait("vector", [tok_small])
            emit("vector", lambda e: e.tensor_tensor(scr3, XH_CUR, XH_SHF, ALU.subtract))
            for c in range(NCH):
                emit("vector", lambda e, c=c, par=par: e.scalar_tensor_tensor(
                    chv(xrb[par], c), chv(scr, c), mu_r_s[:, c:c + 1], xh3[:, c, 0:ST],
                    ALU.mult, ALU.add))
            for c in range(NCH):
                if c == NCH - 1:
                    emit("vector", lambda e, c=c, par=par: e.scalar_tensor_tensor(
                        chv(xkb[par], c), chv(scr, c), mu_k_s[:, c:c + 1], xh3[:, c, 0:ST],
                        ALU.mult, ALU.add).then_inc(sem_v, 1))
                else:
                    emit("vector", lambda e, c=c, par=par: e.scalar_tensor_tensor(
                        chv(xkb[par], c), chv(scr, c), mu_k_s[:, c:c + 1], xh3[:, c, 0:ST],
                        ALU.mult, ALU.add))
            tok_mix = bump("v")
            emit("vector", lambda e: e.tensor_copy(xh3[:, :, 0:1], xh3[:, :, ST:ST + 1]))
            return tok_mix

        ps_pool = [ps_rot[0], ps_rot[1], ps_x, ps_s[0], ps_s[1], ps_s[2]]
        nbanks = [2]

        def unit_mms(dch, lhsT_of_c, rhs_of_c, wtoks):
            bk = dch % nbanks[0]
            wait("tensor", wtoks + [bank_free[bk]])
            for c in range(NCH):
                if c == NCH - 1:
                    emit("tensor", lambda e, c=c, bk=bk: e.matmul(
                        ps_pool[bk][:, 0:ST], lhsT_of_c(c), rhs_of_c(c),
                        start=(c == 0), stop=True).then_inc(sem_t, 1))
                else:
                    emit("tensor", lambda e, c=c, bk=bk: e.matmul(
                        ps_pool[bk][:, 0:ST], lhsT_of_c(c), rhs_of_c(c),
                        start=(c == 0), stop=False))
            return bk, bump("t")

        def ffn_r(par, tok_mix, interleave=None):
            inter = list(interleave) if interleave else []
            for d in range(NCH):
                if inter:
                    inter.pop(0)()
                wt = [("v", tok_mix), tok_wr] if d == 0 else []
                bk, tok = unit_mms(
                    d, lambda c, d=d: w_r[:, c * 1024 + d * 128:c * 1024 + (d + 1) * 128],
                    lambda c, par=par: chv(xrb[par], c), wt)
                wait("scalar", [("t", tok)] + ([rbuf_guard[0]] if d == 0 else []))
                emit("scalar", lambda e, d=d, bk=bk: e.activation(
                    chv(rbuf, d), ps_pool[bk][:, 0:ST], ACT.Sigmoid).then_inc(sem_s, 1))
                bank_free[bk] = ("s", bump("s"))

        def ffn_k(par):
            tok_sq_last = 0
            for d in range(NCH):
                wt = [tok_wk] if d == 0 else []
                bk, tok = unit_mms(
                    d, lambda c, d=d: w_k[:, c * 1024 + d * 128:c * 1024 + (d + 1) * 128],
                    lambda c, par=par: chv(xkb[par], c), wt)
                wait("scalar", [("t", tok)])
                emit("scalar", lambda e, d=d, bk=bk: e.activation(
                    chv(vcp, d), ps_pool[bk][:, 0:ST], ACT.Relu).then_inc(sem_s, 1))
                bank_free[bk] = ("s", bump("s"))
                emit("scalar", lambda e, d=d: e.activation(
                    chv(kact, d), chv(vcp, d), ACT.Square).then_inc(sem_s, 1))
                tok_sq_last = bump("s")
            return tok_sq_last

        def ffn_v(tok_sq_last, resid, out_buf, resid_dep, out_guard, tt_gp=None):
            cp_toks = []
            for d in range(NCH):
                wt = [("s", tok_sq_last), tok_wv] if d == 0 else []
                bk, tok = unit_mms(
                    d, lambda c, d=d: w_v[:, c * 1024 + d * 128:c * 1024 + (d + 1) * 128],
                    lambda c: chv(kact, c), wt)
                wait("scalar", [("t", tok)])
                emit("scalar", lambda e, d=d, bk=bk: e.copy(
                    chv(vcp, d), ps_pool[bk][:, 0:ST]).then_inc(sem_s, 1))
                cp_toks.append(bump("s"))
                bank_free[bk] = ("s", cp_toks[-1])
            rsrc = out_buf if resid is None else resid
            if tt_gp is not None:
                tts = []
                wait("gpsimd", [tt_gp])
                for d in range(NCH):
                    wait("gpsimd", [("s", cp_toks[d])])
                    emit("gpsimd", lambda e, d=d: e.tensor_tensor(
                        chv(scr, d), chv(rbuf, d), chv(vcp, d), ALU.mult).then_inc(sem_p, 1))
                    tts.append(bump("p"))
                wait("vector", resid_dep + [out_guard])
                for d in range(NCH):
                    wait("vector", [("p", tts[d])])
                    if d == NCH - 1:
                        emit("vector", lambda e, d=d: e.scalar_tensor_tensor(
                            chv(out_buf, d), chv(scr, d), 0.5, chv(rsrc, d),
                            ALU.mult, ALU.add).then_inc(sem_v, 1))
                    else:
                        emit("vector", lambda e, d=d: e.scalar_tensor_tensor(
                            chv(out_buf, d), chv(scr, d), 0.5, chv(rsrc, d),
                            ALU.mult, ALU.add))
                tok_out = bump("v")
                rbuf_guard[0] = ("p", tts[-1])
                vcp_guard[0] = ("p", tts[-1])
                return tok_out
            wait("vector", [("s", cp_toks[-1])] + resid_dep + [out_guard])
            for d in range(NCH):
                emit("vector", lambda e, d=d: e.tensor_mul(chv(scr, d), chv(rbuf, d), chv(vcp, d)))
                if d == NCH - 1:
                    emit("vector", lambda e, d=d, rsrc=rsrc: e.scalar_tensor_tensor(
                        chv(out_buf, d), chv(scr, d), 0.5, chv(rsrc, d),
                        ALU.mult, ALU.add).then_inc(sem_v, 1))
                else:
                    emit("vector", lambda e, d=d, rsrc=rsrc: e.scalar_tensor_tensor(
                        chv(out_buf, d), chv(scr, d), 0.5, chv(rsrc, d),
                        ALU.mult, ALU.add))
            tok_out = bump("v")
            rbuf_guard[0] = ("v", tok_out)
            vcp_guard[0] = ("v", tok_out)
            return tok_out

        tok_xld = [0] * (NST + 3)
        tok_xld[0] = sload(ring[0][:], x_st[0], "r0")
        tok_xld[1] = sload(ring[1][:], x_st[1], "r1")
        tok_xld[2] = sload(ring[2][:], x_st[2], "r2")
        tok_mix_a1 = [0] * NST
        tok_u_all = [0] * NST
        tok_uspill = [0] * NST

        nbanks[0] = 6
        preps = ln_prep(ring[0], [tok_xld[0]])
        for j in range(NST + 1):
            if j < NST:
                tv = ln_statsmm(preps)
            if j >= 1:
                ffn_r((j - 1) % 2, tok_mix_a1[j - 1])
            if j < NST:
                tok_xh = ln_norm(ring[j % 3], tv, True)
                tok_mix_a1[j] = mix_front(j % 2, tok_xh)
                stg[0] += 1
            if j >= 1:
                tok_sq = ffn_k((j - 1) % 2)
            if j + 1 < NST:
                preps = ln_prep(ring[(j + 1) % 3], [tok_xld[j + 1]])
            if j >= 1:
                i = j - 1
                guard = ("us", tok_uspill[i - 1]) if i >= 1 else ("us", 0)
                tok_u_all[i] = ffn_v(tok_sq, ring[i % 3], ubuf, [], guard,
                                     tt_gp=("v", tok_mix_a1[min(j, NST - 1)]))
                wait("gpsimd", [("v", tok_u_all[i])])
                emit("gpsimd", lambda e, i=i: e.dma_start(out=u_scr[i], in_=ubuf[:]).then_inc(sem_us, 16))
                tok_uspill[i] = bump("us", 16)
                if i + 3 < NST:
                    wait("sync", [("v", tok_u_all[i])])
                    tok_xld[i + 3] = sload(ring[i % 3][:], x_st[i + 3], "r" + str(i % 3))

        nbanks[0] = 3
        s_bank_guards = [bank_free[3], bank_free[4], bank_free[5]]
        tok_uld = [0] * (NST + 3)
        wait("sync", [("us", 16 * NST), ("v", tok_u_all[NST - 1])])
        tok_uld[0] = sload(ring[0][:], u_scr[0], "r0")
        tok_uld[1] = sload(ring[1][:], u_scr[1], "r1")
        tok_uld[2] = sload(ring[2][:], u_scr[2], "r2")

        tok_x2 = [0] * NST
        tok_qspill = [0] * NST
        tok_smm = [0] * (2 * NST)
        tok_kv_rhs_last = 0

        for j in range(NST + 1):
            if j >= 1:
                i = j - 1
                par = i % 2
                tok_q_last = 0
                for d in range(NCH):
                    wt = [("v", tok_x2[i]), tok_wqkv] if d == 0 else []
                    bk, tok = unit_mms(
                        d, lambda c, d=d: wsp[:, c * 3072 + d * 128:c * 3072 + (d + 1) * 128],
                        lambda c, par=par: chv(bfA[par], c), wt)
                    wait("scalar", [("t", tok)] + ([tok_qspill[i - 2]] if (d == 0 and i >= 2) else []))
                    emit("scalar", lambda e, d=d, bk=bk, par=par: e.activation(
                        chv(qbf[par], d), ps_pool[bk][:, 0:ST], ACT.Relu).then_inc(sem_s, 1))
                    tok_q_last = bump("s")
                    bank_free[bk] = ("s", tok_q_last)
                wait("gpsimd", [("s", tok_q_last)])
                emit("gpsimd", lambda e, i=i, par=par, sm=(sem_qs0 if i % 2 == 0 else sem_qs1): e.dma_start(
                    out=q_scr[i], in_=qbf[par][:]).then_inc(sm, 16))
                tok_qspill[i] = ("qs" + str(i % 2), bump("qs" + str(i % 2), 16))
            if j < NST:
                if j == 0:
                    wait("vector", [("us", 16 * NST)])
                    preps2 = ln_prep(ring[0], [tok_uld[0]])
                tv = ln_statsmm(preps2)
                tok_x2[j] = ln_norm(ring[j % 3], tv, False, dstA=bfA[j % 2])
                stg[0] += 1
            if j >= 1:
                i = j - 1
                par = i % 2
                for h in range(2):
                    gi = 2 * i + h
                    tok_kv_last = 0
                    for n in range(4):
                        bk = n % 2
                        wait("tensor", [bank_free[bk]])
                        for c in range(NCH):
                            lh = bfA[par][:, c * ST + h * 128:c * ST + (h + 1) * 128]
                            rh = wsp[:, c * 3072 + 1024 + n * 512:c * 3072 + 1024 + (n + 1) * 512]
                            if c == NCH - 1:
                                emit("tensor", lambda e, lh=lh, rh=rh, bk=bk: e.matmul(
                                    ps_rot[bk][:], lh, rh, start=False, stop=True).then_inc(sem_t, 1))
                            elif c == 0:
                                emit("tensor", lambda e, lh=lh, rh=rh, bk=bk: e.matmul(
                                    ps_rot[bk][:], lh, rh, start=True, stop=False))
                            else:
                                emit("tensor", lambda e, lh=lh, rh=rh, bk=bk: e.matmul(
                                    ps_rot[bk][:], lh, rh, start=False, stop=False))
                        tok_mm = bump("t")
                        if n == 3:
                            tok_kv_rhs_last = tok_mm
                        guard = [("t", tok_smm[gi - 1])] if (n == 0 and gi >= 1) else []
                        wait("scalar", [("t", tok_mm)] + guard)
                        if n < 2:
                            emit("scalar", lambda e, n=n, bk=bk: e.activation(
                                ktm[:, n * 512:(n + 1) * 512], ps_rot[bk][:], ACT.Relu).then_inc(sem_s, 1))
                        else:
                            dstv = vpad[:].rearrange("p (hh e) -> p hh e", e=33)[:, (n - 2) * 16:(n - 1) * 16, 0:32]
                            emit("scalar", lambda e, bk=bk, dstv=dstv: e.activation(
                                dstv, ps_rot[bk][:].rearrange("p (hh e) -> p hh e", e=32),
                                ACT.Relu).then_inc(sem_s, 1))
                        tok_kv_last = bump("s")
                        bank_free[bk] = ("s", tok_kv_last)
                    wait("tensor", [("s", tok_kv_last)] + (s_bank_guards if gi == 0 else []))
                    for g in range(8):
                        pst = ps_s[g // 3]
                        col = (g % 3) * 132
                        start = (gi == 0 and g % 3 == 0)
                        stop = (gi == 2 * NST - 1)
                        if g == 7:
                            emit("tensor", lambda e, g=g, pst=pst, col=col, start=start, stop=stop: e.matmul(
                                pst[:, col:col + 132], ktm[:, g * 128:(g + 1) * 128],
                                vpad[:, g * 132:(g + 1) * 132], start=start, stop=stop,
                                skip_group_check=True).then_inc(sem_t, 1))
                        else:
                            emit("tensor", lambda e, g=g, pst=pst, col=col, start=start, stop=stop: e.matmul(
                                pst[:, col:col + 132], ktm[:, g * 128:(g + 1) * 128],
                                vpad[:, g * 132:(g + 1) * 132], start=start, stop=stop,
                                skip_group_check=True))
                    tok_smm[gi] = bump("t")

                if i + 3 < NST:
                    wait("sync", [("v", tok_x2[i])])
                    tok_uld[i + 3] = sload(ring[i % 3][:], u_scr[i + 3], "r" + str(i % 3))
            if j < NST and j + 1 < NST:
                preps2 = ln_prep(ring[(j + 1) % 3], [tok_uld[j + 1]])

        wait("vector", [("t", tok_smm[2 * NST - 1]), tok_bmask])
        for g in range(8):
            pst = ps_s[g // 3]
            col = (g % 3) * 132
            for jj in range(4):
                emit("vector", lambda e, g=g, jj=jj, pst=pst, col=col: e.tensor_copy(
                    s_app[32 * jj:32 * (jj + 1), g * 128 + 32 * jj:g * 128 + 32 * (jj + 1)],
                    pst[32 * jj:32 * (jj + 1), col + 33 * jj:col + 33 * jj + 32]))
                emit("vector", lambda e, g=g, jj=jj, pst=pst, col=col: e.tensor_copy(
                    dv[32 * jj:32 * (jj + 1), g:g + 1],
                    pst[32 * jj:32 * (jj + 1), col + 33 * jj + 32:col + 33 * jj + 33]))
        for g in range(8):
            if g == 7:
                emit("vector", lambda e, g=g: e.tensor_scalar(
                    denrep[:, g * 128:(g + 1) * 128], bmask[:], dv[:, g:g + 1],
                    None, ALU.mult).then_inc(sem_v, 1))
            else:
                emit("vector", lambda e, g=g: e.tensor_scalar(
                    denrep[:, g * 128:(g + 1) * 128], bmask[:], dv[:, g:g + 1],
                    None, ALU.mult))
        tok_sevac = bump("v")

        wait("gpsimd", [("t", tok_kv_rhs_last)])
        gload(wsp[:, 0:NCH * 1024], w_p_d)
        tok_wp = ("w", 80)

        tok_uld2 = [0] * (NST + 2)
        tok_qld = [0] * (NST + 2)
        wait("sync", [("v", tok_x2[NST - 1])])
        tok_uld2[0] = sload(xbuf[0][:], u_scr[0], "r0")
        wait("sync", [("v", tok_x2[NST - 3])])
        tok_uld2[1] = sload(xbuf[1][:], u_scr[1], "r1")
        tok_wbuf_free = ("v", tok_x2[NST - 2])
        wait("sync", [("qs0", ctr["qs0"]), ("qs1", ctr["qs1"])])
        tok_qld[0] = sload(qbf[0][:], q_scr[0], "q0")
        tok_qld[1] = sload(qbf[1][:], q_scr[1], "q1")

        emit("vector", lambda e: e.memset(xh3[:, :, 0:1], 0.0))

        app_bank_free = None
        tok_div = [[0] * 8 for _ in range(NST)]
        tok_mix_b = [0] * NST
        tok_w_all = [0] * NST
        tok_zst = [0] * NST
        tok_zprev = [0]

        tok_app = [[0] * 8 for _ in range(NST)]

        def emit_app_mm(i, g):
            par = i % 2
            bk = g % 3
            pst = ps_s[bk]
            if g >= 3:
                need = 8 * i + (g - 3) + 1
            elif i >= 1:
                need = 8 * (i - 1) + {0: 6, 1: 7, 2: 5}[bk] + 1
            else:
                need = 0
            wt = [("d", need)]
            if g == 0:
                wt += [tok_qld[i]]
            if i == 0:
                wt += [("v", tok_sevac)]
            wait("tensor", wt)
            emit("tensor", lambda e, g=g, pst=pst, par=par: e.matmul(
                pst[:, 0:ST], s_app[:, g * 128:(g + 1) * 128], chv(qbf[par], g),
                start=True, stop=True, skip_group_check=True))
            emit("tensor", lambda e, g=g, pst=pst, par=par: e.matmul(
                pst[:, ST:2 * ST], denrep[:, g * 128:(g + 1) * 128], chv(qbf[par], g),
                start=False, stop=False, skip_group_check=True))
            emit("tensor", lambda e, pst=pst: e.matmul(
                pst[:, ST:2 * ST], ones_row[:], epsA_row[:],
                start=False, stop=True, skip_group_check=True).then_inc(sem_t, 1))
            tok_app[i][g] = bump("t")

        def den_div(i):
            par = i % 2
            for g in range(8):
                pst = ps_s[g % 3]
                half = (g % 2) * ST
                wt = [("t", tok_app[i][g])]
                if g >= 2:
                    wt.append(("d", 8 * i + g - 1))
                wait("scalar", wt)
                emit("scalar", lambda e, pst=pst: e.activation(
                    ts2[:], pst[:, ST:2 * ST], ACT.Ln))
                emit("scalar", lambda e, half=half: e.activation(
                    drec[:, half:half + ST], ts2[:], ACT.Exp,
                    scale=cst[:, 1:2]).then_inc(sem_s, 1))
                tok_dr = bump("s")
                wait("vector", [("s", tok_dr)])
                emit("vector", lambda e, g=g, pst=pst, par=par, half=half: e.tensor_mul(
                    chv(bfA[par], g), pst[:, 0:ST], drec[:, half:half + ST]).then_inc(sem_d, 1))
                bump("d")
                tok_div[i][g] = ("d", 8 * i + g + 1)

        for g in range(8):
            emit_app_mm(0, g)
        den_div(0)

        pp = [wbuf, ubuf]
        for j in range(NST + 1):
            i = j - 1
            if j < NST:
                par = j % 2
                tok_wev = 0
                for d in range(NCH):
                    wt = [tok_div[j][7], tok_wp] if d == 0 else []
                    bk, tok = unit_mms(
                        d, lambda c, d=d: wsp[:, c * 1024 + d * 128:c * 1024 + (d + 1) * 128],
                        lambda c, par=par: chv(bfA[par], c), wt)
                    wextra = []
                    if d == 0:
                        wextra = [tok_uld2[j]]
                        if j == 0:
                            wextra.append(tok_wbuf_free)
                        elif j == 1:
                            wextra.append(("v", tok_x2[NST - 1]))
                        else:
                            wextra.append(("zs", tok_zst[j - 2]))
                    wait("vector", [("t", tok)] + wextra)
                    emit("vector", lambda e, d=d, bk=bk, par=par, j=j: e.scalar_tensor_tensor(
                        chv(pp[j % 2], d), ps_pool[bk][:, 0:ST], bias_s[:, d:d + 1],
                        chv(xbuf[par], d), ALU.add, ALU.add).then_inc(sem_v, 1))
                    tok_wev = bump("v")
                    bank_free[bk] = ("v", tok_wev)
                tok_w_all[j] = tok_wev
            if j < NST:
                preps3 = ln_prep(pp[j % 2], [("v", tok_w_all[j])])
            if j >= 1:
                ffn_r(i % 2, tok_mix_b[i])
            if j < NST:
                tvb = ln_statsmm(preps3)
                tok_xh3 = ln_norm(pp[j % 2], tvb, True)
            if j >= 1:
                tok_sq = ffn_k(i % 2)
            if j < NST and j + 1 < NST:
                for g in range(8):
                    emit_app_mm(j + 1, g)
                den_div(j + 1)
            if j >= 1:
                tok_z = ffn_v(tok_sq, None, pp[i % 2], [], ("v", 0),
                              tt_gp=("v", tok_mix_b[i]))
                tok_zprev[0] = tok_z
                wait("gpsimd", [("v", tok_z)])
                emit("gpsimd", lambda e, i=i: e.dma_start(
                    out=z_out[i], in_=pp[i % 2][:]).then_inc(sem_zs, 16))
                tok_zst[i] = bump("zs", 16)
            if j < NST:
                tok_mix_b[j] = mix_front(j % 2, tok_xh3)
                stg[0] += 1
            if j < NST:
                if j + 2 < NST:
                    wait("sync", [("v", tok_w_all[j])])
                    tok_uld2[j + 2] = sload(xbuf[j % 2][:], u_scr[j + 2], "r" + str(j % 2))
                    wait("sync", [tok_div[j][7]])
                    tok_qld[j + 2] = sload(qbf[j % 2][:], q_scr[j + 2], "q" + str(j % 2))

        wait("gpsimd", [(k, ctr[k]) for k in ("g", "w", "us", "qs0", "qs1", "zs")])
        wait("sync", [(k, ctr[k]) for k in ("x", "r0", "r1", "r2", "q0", "q1")])

        with nc.Block() as block:
            @block.sync
            def _(sync):
                for f in ops["sync"]:
                    f(sync)

            @block.gpsimd
            def _(gpsimd):
                for f in ops["gpsimd"]:
                    f(gpsimd)

            @block.tensor
            def _(tensor):
                for f in ops["tensor"]:
                    f(tensor)

            @block.vector
            def _(vector):
                for f in ops["vector"]:
                    f(vector)

            @block.scalar
            def _(scalar):
                for f in ops["scalar"]:
                    f(scalar)

    return nc


def _stage_fm(mat_t):
    Dd = mat_t.shape[1]
    return np.ascontiguousarray(
        mat_t.reshape(NCH, 128, Dd).transpose(1, 0, 2).reshape(128, NCH * Dd))


def kernel(L0_attn_x, L0_attn_qkv_weight, L0_attn_proj_weight, L0_attn_proj_bias,
           L0_ffn_W_r, L0_ffn_W_k, L0_ffn_W_v, L0_ffn_mu_r, L0_ffn_mu_k):
    x = np.ascontiguousarray(np.asarray(L0_attn_x, dtype=np.float32))
    w_qkv = _stage_fm(np.ascontiguousarray(np.asarray(L0_attn_qkv_weight, np.float32).T))
    w_p = _stage_fm(np.ascontiguousarray(np.asarray(L0_attn_proj_weight, np.float32).T))
    w_r = _stage_fm(np.ascontiguousarray(np.asarray(L0_ffn_W_r, np.float32).T))
    w_k = _stage_fm(np.ascontiguousarray(np.asarray(L0_ffn_W_k, np.float32).T))
    w_v = _stage_fm(np.ascontiguousarray(np.asarray(L0_ffn_W_v, np.float32).T))
    mu_r = np.ascontiguousarray(np.asarray(L0_ffn_mu_r, np.float32).reshape(NCH, 128).T)
    mu_k = np.ascontiguousarray(np.asarray(L0_ffn_mu_k, np.float32).reshape(NCH, 128).T)
    bias = np.ascontiguousarray(np.asarray(L0_attn_proj_bias, np.float32).reshape(NCH, 128).T)
    bmask = np.kron(np.eye(4, dtype=np.float32), np.ones((32, 32), np.float32))

    nc = build()
    in_maps = []
    for b in range(B):
        xs = np.ascontiguousarray(
            x[b].T.reshape(NCH, 128, NST, ST).transpose(2, 1, 0, 3).reshape(NST, 128, NCH * ST))
        in_maps.append({
            "x_st": xs, "w_r_d": w_r, "w_k_d": w_k, "w_v_d": w_v,
            "w_qkv_d": w_qkv, "w_p_d": w_p, "mu_r_d": mu_r, "mu_k_d": mu_k,
            "bias_d": bias, "bmask_d": bmask,
        })
    res = run_bass_kernel_spmd(nc, in_maps, core_ids=list(range(B)))
    out = np.empty((B, N, C), dtype=np.float32)
    for b in range(B):
        z = np.asarray(res.results[b]["z_out"])
        out[b] = z.reshape(NST, 128, NCH, ST).transpose(0, 3, 2, 1).reshape(N, C)
    return out


# version 40 (final)
# speedup vs baseline: 1.0227x; 1.0121x over previous
import numpy as np
import concourse.bass as bass
import concourse.mybir as mybir
from concourse.bass_utils import run_bass_kernel_spmd

F32 = mybir.dt.float32
BF16 = mybir.dt.bfloat16
ALU = mybir.AluOpType
ACT = mybir.ActivationFunctionType

B, N, C = 8, 4096, 1024
ST = 256
NST = N // ST
NCH = C // 128
LN_EPS = 1e-5
ATT_EPS = 1e-15


def build(debug=False):
    nc = bass.Bass("TRN2", target_bir_lowering=False, detect_race_conditions=False)

    x_st = nc.dram_tensor("x_st", [NST, 128, NCH * ST], F32, kind="ExternalInput").ap()
    w_r_d = nc.dram_tensor("w_r_d", [128, NCH * 1024], F32, kind="ExternalInput").ap()
    w_k_d = nc.dram_tensor("w_k_d", [128, NCH * 1024], F32, kind="ExternalInput").ap()
    w_v_d = nc.dram_tensor("w_v_d", [128, NCH * 1024], F32, kind="ExternalInput").ap()
    w_qkv_d = nc.dram_tensor("w_qkv_d", [128, NCH * 3072], F32, kind="ExternalInput").ap()
    w_p_d = nc.dram_tensor("w_p_d", [128, NCH * 1024], F32, kind="ExternalInput").ap()
    mu_r_d = nc.dram_tensor("mu_r_d", [128, NCH], F32, kind="ExternalInput").ap()
    mu_k_d = nc.dram_tensor("mu_k_d", [128, NCH], F32, kind="ExternalInput").ap()
    bias_d = nc.dram_tensor("bias_d", [128, NCH], F32, kind="ExternalInput").ap()
    bmask_d = nc.dram_tensor("bmask_d", [128, 128], F32, kind="ExternalInput").ap()

    sk = "ExternalOutput" if debug else "Internal"
    u_scr = nc.dram_tensor("u_scr", [NST, 128, NCH * ST], F32, kind=sk).ap()
    q_scr = nc.dram_tensor("q_scr", [NST, 128, NCH * ST], BF16, kind=sk).ap()
    z_out = nc.dram_tensor("z_out", [NST, 128, NCH * ST], F32, kind="ExternalOutput").ap()

    from contextlib import ExitStack
    with ExitStack() as _es:
        def SB(name, shape, dt):
            return _es.enter_context(nc.sbuf_tensor(name, shape, dt))

        def PSUM(name, shape):
            return _es.enter_context(nc.psum_tensor(name, shape, F32))

        w_r = SB("w_r", [128, NCH * 1024], BF16)
        w_k = SB("w_k", [128, NCH * 1024], BF16)
        w_v = SB("w_v", [128, NCH * 1024], BF16)
        wsp = SB("wsp", [128, NCH * 3072], BF16)
        xbuf = [SB("xbuf0", [128, NCH * ST], F32), SB("xbuf1", [128, NCH * ST], F32)]
        ubuf = SB("ubuf", [128, NCH * ST], F32)
        wbuf = SB("wbuf", [128, NCH * ST], F32)
        xh = SB("xh", [128, NCH * 260], F32)
        scr = SB("scr", [128, NCH * ST], BF16)
        rbuf = SB("rbuf", [128, NCH * ST], BF16)
        vcp = SB("vcp", [128, NCH * ST], BF16)
        xrb = [SB("xr0", [128, NCH * ST], BF16), SB("xr1", [128, NCH * ST], BF16)]
        xkb = [SB("xk0", [128, NCH * ST], BF16), SB("xk1", [128, NCH * ST], BF16)]
        xb16 = SB("xb16", [128, NCH * ST], BF16)
        xsq = SB("xsq", [128, NCH * ST], BF16)
        kact = SB("kact", [128, NCH * ST], BF16)
        bfA = [SB("bfA0", [128, NCH * ST], BF16), SB("bfA1", [128, NCH * ST], BF16)]
        qbf = [SB("qb0", [128, NCH * ST], BF16), SB("qb1", [128, NCH * ST], BF16)]
        ktm = SB("ktm", [128, 1024], BF16)
        vpad = SB("vpad", [128, 1056], BF16)
        mu_r_s = SB("mu_r_s", [128, NCH], F32)
        mu_k_s = SB("mu_k_s", [128, NCH], F32)
        bias_s = SB("bias_s", [128, NCH], F32)
        bmask = SB("bmask", [128, 128], BF16)
        ones_l = SB("ones_l", [128, 128], BF16)
        ones_row = SB("ones_row", [1, 128], BF16)
        epsA_row = SB("epsA_row", [1, ST], BF16)
        cst = SB("cst", [128, 5], F32)
        ts1 = SB("ts1", [128, ST], F32)
        ts2 = SB("ts2", [128, ST], F32)
        drec = SB("drec", [128, 2 * ST], F32)
        rs_s = SB("rs_s", [128, ST], F32)
        m_s = SB("m_s", [128, ST], F32)
        s_app = SB("s_app", [128, 8 * 128], BF16)
        denrep = SB("denrep", [128, 8 * 128], BF16)
        dv = SB("dv", [128, 8], F32)

        ps_st = [PSUM("ps_st0", [128, 512]), PSUM("ps_st1", [128, 512])]
        ps_rot = [PSUM("ps_rot0", [128, 512]), PSUM("ps_rot1", [128, 512])]
        ps_s = [PSUM("ps_s0", [128, 512]), PSUM("ps_s1", [128, 512]), PSUM("ps_s2", [128, 512])]
        ps_x = PSUM("ps_x", [128, 512])

        sem_x = _es.enter_context(nc.semaphore("sem_x"))
        sem_g = _es.enter_context(nc.semaphore("sem_g"))
        sem_t = _es.enter_context(nc.semaphore("sem_t"))
        sem_v = _es.enter_context(nc.semaphore("sem_v"))
        sem_s = _es.enter_context(nc.semaphore("sem_s"))
        sem_d = _es.enter_context(nc.semaphore("sem_d"))
        sem_r0 = _es.enter_context(nc.semaphore("sem_r0"))
        sem_r1 = _es.enter_context(nc.semaphore("sem_r1"))
        sem_r2 = _es.enter_context(nc.semaphore("sem_r2"))
        sem_q0 = _es.enter_context(nc.semaphore("sem_q0"))
        sem_q1 = _es.enter_context(nc.semaphore("sem_q1"))
        sem_w = _es.enter_context(nc.semaphore("sem_w"))
        sem_us = _es.enter_context(nc.semaphore("sem_us"))
        sem_qs0 = _es.enter_context(nc.semaphore("sem_qs0"))
        sem_qs1 = _es.enter_context(nc.semaphore("sem_qs1"))
        sem_zs = _es.enter_context(nc.semaphore("sem_zs"))
        sem_p = _es.enter_context(nc.semaphore("sem_p"))

        SEMS = {"x": sem_x, "g": sem_g, "t": sem_t, "v": sem_v, "s": sem_s, "d": sem_d,
                "r0": sem_r0, "r1": sem_r1, "r2": sem_r2, "q0": sem_q0, "q1": sem_q1,
                "w": sem_w, "us": sem_us, "qs0": sem_qs0, "qs1": sem_qs1, "zs": sem_zs,
                "p": sem_p}
        ops = {e: [] for e in ("sync", "gpsimd", "tensor", "vector", "scalar")}
        ctr = {k: 0 for k in SEMS}

        def emit(eng, fn):
            ops[eng].append(fn)

        def wait(eng, toks):
            for key, val in toks:
                if val > 0:
                    emit(eng, lambda e, s=SEMS[key], v=val: e.wait_ge(s, v))

        def bump(key, by=1):
            ctr[key] += by
            return ctr[key]

        def chv(t, ch, w=ST):
            return t[:, ch * w:(ch + 1) * w]

        xh3 = xh[:].rearrange("p (c w) -> p c w", w=260)
        XH_CUR = xh3[:, :, 1:ST + 1]
        XH_SHF = xh3[:, :, 0:ST]

        ring = [xbuf[0], xbuf[1], wbuf]
        bank_free = {0: ("v", 0), 1: ("v", 0), 2: ("v", 0), 3: ("v", 0), 4: ("v", 0), 5: ("v", 0)}
        stats_free = {0: ("v", 0), 1: ("v", 0)}
        rs_guard = [("v", 0)]
        prep_guard = [("t", 0)]
        rbuf_guard = [("v", 0)]
        vcp_guard = [("v", 0)]
        stg = [0]

        emit("vector", lambda e: e.memset(ones_l[:], 1.0 / 1024.0))
        emit("vector", lambda e: e.memset(ones_row[:], 1.0))
        emit("vector", lambda e: e.memset(epsA_row[:], ATT_EPS))
        emit("vector", lambda e: e.memset(xh[:], 0.0))
        emit("vector", lambda e: e.memset(s_app[:], 0.0))
        emit("vector", lambda e: e.memset(cst[:, 0:1], -0.5))
        emit("vector", lambda e: e.memset(cst[:, 1:2], -1.0))
        emit("vector", lambda e: e.memset(cst[:, 2:3], ATT_EPS))
        emit("vector", lambda e: e.memset(cst[:, 3:4], 0.70710678))
        emit("vector", lambda e: e.memset(cst[:, 4:5], LN_EPS))
        emit("vector", lambda e: e.memset(vpad[:], 1.0).then_inc(sem_v, 1))
        tok_init = bump("v")

        def gload(dst, src, key="w"):
            emit("gpsimd", lambda e, d=dst, s=src, sm=SEMS[key]: e.dma_start(out=d, in_=s).then_inc(sm, 16))
            return key, bump(key, 16)

        def sload(dst, src, key="x"):
            emit("sync", lambda e, d=dst, s=src, sm=SEMS[key]: e.dma_start(out=d, in_=s).then_inc(sm, 16))
            return key, bump(key, 16)

        gload(w_r[:], w_r_d)
        gload(w_k[:], w_k_d)
        gload(w_v[:], w_v_d)
        gload(wsp[:], w_qkv_d)
        tok_wr = tok_wk = tok_wv = tok_wqkv = ("w", 64)

        sload(mu_r_s[:], mu_r_d)
        sload(mu_k_s[:], mu_k_d)
        sload(bias_s[:], bias_d)
        tok_bmask = gload(bmask[:], bmask_d, "g")
        tok_small = ("x", ctr["x"])

        def ln_prep(src, deps):
            wait("scalar", deps + [prep_guard[0]])
            emit("scalar", lambda e: e.activation(xsq[:], src[:], ACT.Square).then_inc(sem_s, 1))
            tok_prep = bump("s")
            wait("vector", deps + [prep_guard[0]])
            emit("vector", lambda e: e.tensor_copy(xb16[:], src[:]).then_inc(sem_v, 1))
            tok_prepv = bump("v")
            return tok_prep, tok_prepv

        def ln_statsmm(preps):
            par = stg[0] % 2
            ps = ps_st[par]
            tok_prep, tok_prepv = preps
            wait("tensor", [("s", tok_prep), ("v", tok_prepv), stats_free[par], ("v", tok_init)])
            for c in range(NCH):
                emit("tensor", lambda e, c=c, ps=ps: e.matmul(
                    ps[:, 0:ST], ones_l[:], chv(xb16, c), start=(c == 0), stop=(c == NCH - 1)))
            for c in range(NCH):
                if c == NCH - 1:
                    emit("tensor", lambda e, c=c, ps=ps: e.matmul(
                        ps[:, ST:2 * ST], ones_l[:], chv(xsq, c),
                        start=False, stop=True, skip_group_check=True).then_inc(sem_t, 1))
                else:
                    emit("tensor", lambda e, c=c, ps=ps: e.matmul(
                        ps[:, ST:2 * ST], ones_l[:], chv(xsq, c),
                        start=False, stop=False, skip_group_check=True))
            tok_stats = bump("t")
            prep_guard[0] = ("t", tok_stats)
            wait("vector", [("t", tok_stats)])
            emit("vector", lambda e, ps=ps: e.tensor_copy(m_s[:], ps[:, 0:ST]))
            emit("vector", lambda e: e.tensor_mul(ts1[:], m_s[:], m_s[:]))
            emit("vector", lambda e, ps=ps: e.scalar_tensor_tensor(
                ts2[:], ts1[:], -1.0, ps[:, ST:2 * ST], ALU.mult, ALU.add).then_inc(sem_v, 1))
            tok_vareps = bump("v")
            stats_free[par] = ("v", tok_vareps)
            return tok_vareps

        def ln_stats(src, deps):
            return ln_statsmm(ln_prep(src, deps))

        def ln_norm(src, tok_vareps, out_xh, dstA=None):
            wait("scalar", [("v", tok_vareps), rs_guard[0]])
            emit("scalar", lambda e: e.activation(ts1[:], ts2[:], ACT.Ln, bias=cst[:, 4:5]))
            emit("scalar", lambda e: e.activation(
                rs_s[:], ts1[:], ACT.Exp, scale=cst[:, 0:1]).then_inc(sem_s, 1))
            tok_rs = bump("s")
            wait("vector", [("s", tok_rs)])
            rs_b = rs_s[:].unsqueeze(1).broadcast_to([128, NCH, ST])
            m_b = m_s[:].unsqueeze(1).broadcast_to([128, NCH, ST])
            src3 = src[:].rearrange("p (c w) -> p c w", w=ST)
            if out_xh:
                emit("vector", lambda e: e.tensor_tensor(XH_CUR, src3, m_b, ALU.subtract))
                emit("vector", lambda e: e.tensor_tensor(XH_CUR, XH_CUR, rs_b, ALU.mult).then_inc(sem_v, 1))
            else:
                u3 = ubuf[:].rearrange("p (c w) -> p c w", w=ST)
                d3 = dstA[:].rearrange("p (c w) -> p c w", w=ST)
                emit("vector", lambda e: e.tensor_tensor(u3, src3, m_b, ALU.subtract))
                emit("vector", lambda e: e.tensor_tensor(d3, u3, rs_b, ALU.mult).then_inc(sem_v, 1))
            tok_done = bump("v")
            rs_guard[0] = ("v", tok_done)
            return tok_done

        def mix_front(par, tok_xh):
            scr3 = scr[:].rearrange("p (c w) -> p c w", w=ST)
            wait("vector", [tok_small])
            emit("vector", lambda e: e.tensor_tensor(scr3, XH_CUR, XH_SHF, ALU.subtract))
            for c in range(NCH):
                emit("vector", lambda e, c=c, par=par: e.scalar_tensor_tensor(
                    chv(xrb[par], c), chv(scr, c), mu_r_s[:, c:c + 1], xh3[:, c, 0:ST],
                    ALU.mult, ALU.add))
            for c in range(NCH):
                if c == NCH - 1:
                    emit("vector", lambda e, c=c, par=par: e.scalar_tensor_tensor(
                        chv(xkb[par], c), chv(scr, c), mu_k_s[:, c:c + 1], xh3[:, c, 0:ST],
                        ALU.mult, ALU.add).then_inc(sem_v, 1))
                else:
                    emit("vector", lambda e, c=c, par=par: e.scalar_tensor_tensor(
                        chv(xkb[par], c), chv(scr, c), mu_k_s[:, c:c + 1], xh3[:, c, 0:ST],
                        ALU.mult, ALU.add))
            tok_mix = bump("v")
            emit("vector", lambda e: e.tensor_copy(xh3[:, :, 0:1], xh3[:, :, ST:ST + 1]))
            return tok_mix

        ps_pool = [ps_rot[0], ps_rot[1], ps_x, ps_s[0], ps_s[1], ps_s[2]]
        nbanks = [2]

        def unit_mms(dch, lhsT_of_c, rhs_of_c, wtoks):
            bk = dch % nbanks[0]
            wait("tensor", wtoks + [bank_free[bk]])
            for c in range(NCH):
                if c == NCH - 1:
                    emit("tensor", lambda e, c=c, bk=bk: e.matmul(
                        ps_pool[bk][:, 0:ST], lhsT_of_c(c), rhs_of_c(c),
                        start=(c == 0), stop=True).then_inc(sem_t, 1))
                else:
                    emit("tensor", lambda e, c=c, bk=bk: e.matmul(
                        ps_pool[bk][:, 0:ST], lhsT_of_c(c), rhs_of_c(c),
                        start=(c == 0), stop=False))
            return bk, bump("t")

        def ffn_r(par, tok_mix, interleave=None):
            inter = list(interleave) if interleave else []
            for d in range(NCH):
                if inter:
                    inter.pop(0)()
                wt = [("v", tok_mix), tok_wr] if d == 0 else []
                bk, tok = unit_mms(
                    d, lambda c, d=d: w_r[:, c * 1024 + d * 128:c * 1024 + (d + 1) * 128],
                    lambda c, par=par: chv(xrb[par], c), wt)
                wait("scalar", [("t", tok)] + ([rbuf_guard[0]] if d == 0 else []))
                emit("scalar", lambda e, d=d, bk=bk: e.activation(
                    chv(rbuf, d), ps_pool[bk][:, 0:ST], ACT.Sigmoid).then_inc(sem_s, 1))
                bank_free[bk] = ("s", bump("s"))

        def ffn_k(par):
            tok_sq_last = 0
            for d in range(NCH):
                wt = [tok_wk] if d == 0 else []
                bk, tok = unit_mms(
                    d, lambda c, d=d: w_k[:, c * 1024 + d * 128:c * 1024 + (d + 1) * 128],
                    lambda c, par=par: chv(xkb[par], c), wt)
                wait("scalar", [("t", tok)])
                emit("scalar", lambda e, d=d, bk=bk: e.activation(
                    chv(vcp, d), ps_pool[bk][:, 0:ST], ACT.Relu).then_inc(sem_s, 1))
                bank_free[bk] = ("s", bump("s"))
                emit("scalar", lambda e, d=d: e.activation(
                    chv(kact, d), chv(vcp, d), ACT.Square).then_inc(sem_s, 1))
                tok_sq_last = bump("s")
            return tok_sq_last

        def ffn_v(tok_sq_last, resid, out_buf, resid_dep, out_guard, tt_gp=None):
            cp_toks = []
            for d in range(NCH):
                wt = [("s", tok_sq_last), tok_wv] if d == 0 else []
                bk, tok = unit_mms(
                    d, lambda c, d=d: w_v[:, c * 1024 + d * 128:c * 1024 + (d + 1) * 128],
                    lambda c: chv(kact, c), wt)
                wait("scalar", [("t", tok)])
                emit("scalar", lambda e, d=d, bk=bk: e.copy(
                    chv(vcp, d), ps_pool[bk][:, 0:ST]).then_inc(sem_s, 1))
                cp_toks.append(bump("s"))
                bank_free[bk] = ("s", cp_toks[-1])
            rsrc = out_buf if resid is None else resid
            if tt_gp is not None:
                tts = []
                wait("gpsimd", [tt_gp])
                for d in range(NCH):
                    wait("gpsimd", [("s", cp_toks[d])])
                    emit("gpsimd", lambda e, d=d: e.tensor_tensor(
                        chv(scr, d), chv(rbuf, d), chv(vcp, d), ALU.mult).then_inc(sem_p, 1))
                    tts.append(bump("p"))
                wait("vector", resid_dep + [out_guard])
                for d in range(NCH):
                    wait("vector", [("p", tts[d])])
                    if d == NCH - 1:
                        emit("vector", lambda e, d=d: e.scalar_tensor_tensor(
                            chv(out_buf, d), chv(scr, d), 0.5, chv(rsrc, d),
                            ALU.mult, ALU.add).then_inc(sem_v, 1))
                    else:
                        emit("vector", lambda e, d=d: e.scalar_tensor_tensor(
                            chv(out_buf, d), chv(scr, d), 0.5, chv(rsrc, d),
                            ALU.mult, ALU.add))
                tok_out = bump("v")
                rbuf_guard[0] = ("p", tts[-1])
                vcp_guard[0] = ("p", tts[-1])
                return tok_out
            wait("vector", [("s", cp_toks[-1])] + resid_dep + [out_guard])
            for d in range(NCH):
                emit("vector", lambda e, d=d: e.tensor_mul(chv(scr, d), chv(rbuf, d), chv(vcp, d)))
                if d == NCH - 1:
                    emit("vector", lambda e, d=d, rsrc=rsrc: e.scalar_tensor_tensor(
                        chv(out_buf, d), chv(scr, d), 0.5, chv(rsrc, d),
                        ALU.mult, ALU.add).then_inc(sem_v, 1))
                else:
                    emit("vector", lambda e, d=d, rsrc=rsrc: e.scalar_tensor_tensor(
                        chv(out_buf, d), chv(scr, d), 0.5, chv(rsrc, d),
                        ALU.mult, ALU.add))
            tok_out = bump("v")
            rbuf_guard[0] = ("v", tok_out)
            vcp_guard[0] = ("v", tok_out)
            return tok_out

        tok_xld = [0] * (NST + 3)
        tok_xld[0] = sload(ring[0][:], x_st[0], "r0")
        tok_xld[1] = sload(ring[1][:], x_st[1], "r1")
        tok_xld[2] = sload(ring[2][:], x_st[2], "r2")
        tok_mix_a1 = [0] * NST
        tok_u_all = [0] * NST
        tok_uspill = [0] * NST

        nbanks[0] = 6
        preps = ln_prep(ring[0], [tok_xld[0]])
        for j in range(NST + 1):
            if j < NST:
                tv = ln_statsmm(preps)
            if j >= 1:
                ffn_r((j - 1) % 2, tok_mix_a1[j - 1])
            if j < NST:
                tok_xh = ln_norm(ring[j % 3], tv, True)
                tok_mix_a1[j] = mix_front(j % 2, tok_xh)
                stg[0] += 1
            if j >= 1:
                tok_sq = ffn_k((j - 1) % 2)
            if j + 1 < NST:
                preps = ln_prep(ring[(j + 1) % 3], [tok_xld[j + 1]])
            if j >= 1:
                i = j - 1
                guard = ("us", tok_uspill[i - 1]) if i >= 1 else ("us", 0)
                tok_u_all[i] = ffn_v(tok_sq, ring[i % 3], ubuf, [], guard)
                wait("gpsimd", [("v", tok_u_all[i])])
                emit("gpsimd", lambda e, i=i: e.dma_start(out=u_scr[i], in_=ubuf[:]).then_inc(sem_us, 16))
                tok_uspill[i] = bump("us", 16)
                if i + 3 < NST:
                    wait("sync", [("v", tok_u_all[i])])
                    tok_xld[i + 3] = sload(ring[i % 3][:], x_st[i + 3], "r" + str(i % 3))

        nbanks[0] = 3
        s_bank_guards = [bank_free[3], bank_free[4], bank_free[5]]
        tok_uld = [0] * (NST + 3)
        wait("sync", [("us", 16 * NST), ("v", tok_u_all[NST - 1])])
        tok_uld[0] = sload(ring[0][:], u_scr[0], "r0")
        tok_uld[1] = sload(ring[1][:], u_scr[1], "r1")
        tok_uld[2] = sload(ring[2][:], u_scr[2], "r2")

        tok_x2 = [0] * NST
        tok_qspill = [0] * NST
        tok_smm = [0] * (2 * NST)
        tok_kv_rhs_last = 0

        for j in range(NST + 1):
            if j >= 1:
                i = j - 1
                par = i % 2
                tok_q_last = 0
                for d in range(NCH):
                    wt = [("v", tok_x2[i]), tok_wqkv] if d == 0 else []
                    bk, tok = unit_mms(
                        d, lambda c, d=d: wsp[:, c * 3072 + d * 128:c * 3072 + (d + 1) * 128],
                        lambda c, par=par: chv(bfA[par], c), wt)
                    wait("scalar", [("t", tok)] + ([tok_qspill[i - 2]] if (d == 0 and i >= 2) else []))
                    emit("scalar", lambda e, d=d, bk=bk, par=par: e.activation(
                        chv(qbf[par], d), ps_pool[bk][:, 0:ST], ACT.Relu).then_inc(sem_s, 1))
                    tok_q_last = bump("s")
                    bank_free[bk] = ("s", tok_q_last)
                wait("gpsimd", [("s", tok_q_last)])
                emit("gpsimd", lambda e, i=i, par=par, sm=(sem_qs0 if i % 2 == 0 else sem_qs1): e.dma_start(
                    out=q_scr[i], in_=qbf[par][:]).then_inc(sm, 16))
                tok_qspill[i] = ("qs" + str(i % 2), bump("qs" + str(i % 2), 16))
            if j < NST:
                if j == 0:
                    wait("vector", [("us", 16 * NST)])
                    preps2 = ln_prep(ring[0], [tok_uld[0]])
                tv = ln_statsmm(preps2)
                tok_x2[j] = ln_norm(ring[j % 3], tv, False, dstA=bfA[j % 2])
                stg[0] += 1
            if j >= 1:
                i = j - 1
                par = i % 2
                for h in range(2):
                    gi = 2 * i + h
                    tok_kv_last = 0
                    for n in range(4):
                        bk = n % 2
                        wait("tensor", [bank_free[bk]])
                        for c in range(NCH):
                            lh = bfA[par][:, c * ST + h * 128:c * ST + (h + 1) * 128]
                            rh = wsp[:, c * 3072 + 1024 + n * 512:c * 3072 + 1024 + (n + 1) * 512]
                            if c == NCH - 1:
                                emit("tensor", lambda e, lh=lh, rh=rh, bk=bk: e.matmul(
                                    ps_rot[bk][:], lh, rh, start=False, stop=True).then_inc(sem_t, 1))
                            elif c == 0:
                                emit("tensor", lambda e, lh=lh, rh=rh, bk=bk: e.matmul(
                                    ps_rot[bk][:], lh, rh, start=True, stop=False))
                            else:
                                emit("tensor", lambda e, lh=lh, rh=rh, bk=bk: e.matmul(
                                    ps_rot[bk][:], lh, rh, start=False, stop=False))
                        tok_mm = bump("t")
                        if n == 3:
                            tok_kv_rhs_last = tok_mm
                        guard = [("t", tok_smm[gi - 1])] if (n == 0 and gi >= 1) else []
                        wait("scalar", [("t", tok_mm)] + guard)
                        if n < 2:
                            emit("scalar", lambda e, n=n, bk=bk: e.activation(
                                ktm[:, n * 512:(n + 1) * 512], ps_rot[bk][:], ACT.Relu).then_inc(sem_s, 1))
                        else:
                            dstv = vpad[:].rearrange("p (hh e) -> p hh e", e=33)[:, (n - 2) * 16:(n - 1) * 16, 0:32]
                            emit("scalar", lambda e, bk=bk, dstv=dstv: e.activation(
                                dstv, ps_rot[bk][:].rearrange("p (hh e) -> p hh e", e=32),
                                ACT.Relu).then_inc(sem_s, 1))
                        tok_kv_last = bump("s")
                        bank_free[bk] = ("s", tok_kv_last)
                    wait("tensor", [("s", tok_kv_last)] + (s_bank_guards if gi == 0 else []))
                    for g in range(8):
                        pst = ps_s[g // 3]
                        col = (g % 3) * 132
                        start = (gi == 0 and g % 3 == 0)
                        stop = (gi == 2 * NST - 1)
                        if g == 7:
                            emit("tensor", lambda e, g=g, pst=pst, col=col, start=start, stop=stop: e.matmul(
                                pst[:, col:col + 132], ktm[:, g * 128:(g + 1) * 128],
                                vpad[:, g * 132:(g + 1) * 132], start=start, stop=stop,
                                skip_group_check=True).then_inc(sem_t, 1))
                        else:
                            emit("tensor", lambda e, g=g, pst=pst, col=col, start=start, stop=stop: e.matmul(
                                pst[:, col:col + 132], ktm[:, g * 128:(g + 1) * 128],
                                vpad[:, g * 132:(g + 1) * 132], start=start, stop=stop,
                                skip_group_check=True))
                    tok_smm[gi] = bump("t")

                if i + 3 < NST:
                    wait("sync", [("v", tok_x2[i])])
                    tok_uld[i + 3] = sload(ring[i % 3][:], u_scr[i + 3], "r" + str(i % 3))
            if j < NST and j + 1 < NST:
                preps2 = ln_prep(ring[(j + 1) % 3], [tok_uld[j + 1]])

        wait("vector", [("t", tok_smm[2 * NST - 1]), tok_bmask])
        for g in range(8):
            pst = ps_s[g // 3]
            col = (g % 3) * 132
            for jj in range(4):
                emit("vector", lambda e, g=g, jj=jj, pst=pst, col=col: e.tensor_copy(
                    s_app[32 * jj:32 * (jj + 1), g * 128 + 32 * jj:g * 128 + 32 * (jj + 1)],
                    pst[32 * jj:32 * (jj + 1), col + 33 * jj:col + 33 * jj + 32]))
                emit("vector", lambda e, g=g, jj=jj, pst=pst, col=col: e.tensor_copy(
                    dv[32 * jj:32 * (jj + 1), g:g + 1],
                    pst[32 * jj:32 * (jj + 1), col + 33 * jj + 32:col + 33 * jj + 33]))
        for g in range(8):
            if g == 7:
                emit("vector", lambda e, g=g: e.tensor_scalar(
                    denrep[:, g * 128:(g + 1) * 128], bmask[:], dv[:, g:g + 1],
                    None, ALU.mult).then_inc(sem_v, 1))
            else:
                emit("vector", lambda e, g=g: e.tensor_scalar(
                    denrep[:, g * 128:(g + 1) * 128], bmask[:], dv[:, g:g + 1],
                    None, ALU.mult))
        tok_sevac = bump("v")

        wait("gpsimd", [("t", tok_kv_rhs_last)])
        gload(wsp[:, 0:NCH * 1024], w_p_d)
        tok_wp = ("w", 80)

        tok_uld2 = [0] * (NST + 2)
        tok_qld = [0] * (NST + 2)
        wait("sync", [("v", tok_x2[NST - 1])])
        tok_uld2[0] = sload(xbuf[0][:], u_scr[0], "r0")
        wait("sync", [("v", tok_x2[NST - 3])])
        tok_uld2[1] = sload(xbuf[1][:], u_scr[1], "r1")
        tok_wbuf_free = ("v", tok_x2[NST - 2])
        wait("sync", [("qs0", ctr["qs0"]), ("qs1", ctr["qs1"])])
        tok_qld[0] = sload(qbf[0][:], q_scr[0], "q0")
        tok_qld[1] = sload(qbf[1][:], q_scr[1], "q1")

        emit("vector", lambda e: e.memset(xh3[:, :, 0:1], 0.0))

        app_bank_free = None
        tok_div = [[0] * 8 for _ in range(NST)]
        tok_mix_b = [0] * NST
        tok_w_all = [0] * NST
        tok_zst = [0] * NST
        tok_zprev = [0]

        tok_app = [[0] * 8 for _ in range(NST)]

        def emit_app_mm(i, g):
            par = i % 2
            bk = g % 3
            pst = ps_s[bk]
            if g >= 3:
                need = 8 * i + (g - 3) + 1
            elif i >= 1:
                need = 8 * (i - 1) + {0: 6, 1: 7, 2: 5}[bk] + 1
            else:
                need = 0
            wt = [("d", need)]
            if g == 0:
                wt += [tok_qld[i]]
            if i == 0:
                wt += [("v", tok_sevac)]
            wait("tensor", wt)
            emit("tensor", lambda e, g=g, pst=pst, par=par: e.matmul(
                pst[:, 0:ST], s_app[:, g * 128:(g + 1) * 128], chv(qbf[par], g),
                start=True, stop=True, skip_group_check=True))
            emit("tensor", lambda e, g=g, pst=pst, par=par: e.matmul(
                pst[:, ST:2 * ST], denrep[:, g * 128:(g + 1) * 128], chv(qbf[par], g),
                start=False, stop=True, skip_group_check=True).then_inc(sem_t, 1))
            tok_app[i][g] = bump("t")

        def den_div(i):
            par = i % 2
            for g in range(8):
                pst = ps_s[g % 3]
                half = (g % 2) * ST
                wt = [("t", tok_app[i][g])]
                if g >= 2:
                    wt.append(("d", 8 * i + g - 1))
                wait("scalar", wt)
                emit("scalar", lambda e, pst=pst: e.activation(
                    ts2[:], pst[:, ST:2 * ST], ACT.Ln, bias=cst[:, 2:3]))
                emit("scalar", lambda e, half=half: e.activation(
                    drec[:, half:half + ST], ts2[:], ACT.Exp,
                    scale=cst[:, 1:2]).then_inc(sem_s, 1))
                tok_dr = bump("s")
                wait("vector", [("s", tok_dr)])
                emit("vector", lambda e, g=g, pst=pst, par=par, half=half: e.tensor_mul(
                    chv(bfA[par], g), pst[:, 0:ST], drec[:, half:half + ST]).then_inc(sem_d, 1))
                bump("d")
                tok_div[i][g] = ("d", 8 * i + g + 1)

        for g in range(8):
            emit_app_mm(0, g)
        den_div(0)

        pp = [wbuf, ubuf]
        for j in range(NST + 1):
            i = j - 1
            if j < NST:
                par = j % 2
                tok_wev = 0
                for d in range(NCH):
                    wt = [tok_div[j][7], tok_wp] if d == 0 else []
                    bk, tok = unit_mms(
                        d, lambda c, d=d: wsp[:, c * 1024 + d * 128:c * 1024 + (d + 1) * 128],
                        lambda c, par=par: chv(bfA[par], c), wt)
                    wextra = []
                    if d == 0:
                        wextra = [tok_uld2[j]]
                        if j == 0:
                            wextra.append(tok_wbuf_free)
                        elif j == 1:
                            wextra.append(("v", tok_x2[NST - 1]))
                        else:
                            wextra.append(("zs", tok_zst[j - 2]))
                    wait("vector", [("t", tok)] + wextra)
                    emit("vector", lambda e, d=d, bk=bk, par=par, j=j: e.scalar_tensor_tensor(
                        chv(pp[j % 2], d), ps_pool[bk][:, 0:ST], bias_s[:, d:d + 1],
                        chv(xbuf[par], d), ALU.add, ALU.add).then_inc(sem_v, 1))
                    tok_wev = bump("v")
                    bank_free[bk] = ("v", tok_wev)
                tok_w_all[j] = tok_wev
            if j < NST:
                preps3 = ln_prep(pp[j % 2], [("v", tok_w_all[j])])
            if j >= 1:
                ffn_r(i % 2, tok_mix_b[i])
            if j < NST:
                tvb = ln_statsmm(preps3)
                tok_xh3 = ln_norm(pp[j % 2], tvb, True)
            if j >= 1:
                tok_sq = ffn_k(i % 2)
            if j < NST and j + 1 < NST:
                for g in range(8):
                    emit_app_mm(j + 1, g)
                den_div(j + 1)
            if j >= 1:
                tok_z = ffn_v(tok_sq, None, pp[i % 2], [], ("v", 0),
                              tt_gp=("v", tok_mix_b[i]))
                tok_zprev[0] = tok_z
                wait("gpsimd", [("v", tok_z)])
                emit("gpsimd", lambda e, i=i: e.dma_start(
                    out=z_out[i], in_=pp[i % 2][:]).then_inc(sem_zs, 16))
                tok_zst[i] = bump("zs", 16)
            if j < NST:
                tok_mix_b[j] = mix_front(j % 2, tok_xh3)
                stg[0] += 1
            if j < NST:
                if j + 2 < NST:
                    wait("sync", [("v", tok_w_all[j])])
                    tok_uld2[j + 2] = sload(xbuf[j % 2][:], u_scr[j + 2], "r" + str(j % 2))
                    wait("sync", [tok_div[j][7]])
                    tok_qld[j + 2] = sload(qbf[j % 2][:], q_scr[j + 2], "q" + str(j % 2))

        wait("gpsimd", [(k, ctr[k]) for k in ("g", "w", "us", "qs0", "qs1", "zs")])
        wait("sync", [(k, ctr[k]) for k in ("x", "r0", "r1", "r2", "q0", "q1")])

        with nc.Block() as block:
            @block.sync
            def _(sync):
                for f in ops["sync"]:
                    f(sync)

            @block.gpsimd
            def _(gpsimd):
                for f in ops["gpsimd"]:
                    f(gpsimd)

            @block.tensor
            def _(tensor):
                for f in ops["tensor"]:
                    f(tensor)

            @block.vector
            def _(vector):
                for f in ops["vector"]:
                    f(vector)

            @block.scalar
            def _(scalar):
                for f in ops["scalar"]:
                    f(scalar)

    return nc


def _stage_fm(mat_t):
    Dd = mat_t.shape[1]
    return np.ascontiguousarray(
        mat_t.reshape(NCH, 128, Dd).transpose(1, 0, 2).reshape(128, NCH * Dd))


def kernel(L0_attn_x, L0_attn_qkv_weight, L0_attn_proj_weight, L0_attn_proj_bias,
           L0_ffn_W_r, L0_ffn_W_k, L0_ffn_W_v, L0_ffn_mu_r, L0_ffn_mu_k):
    x = np.ascontiguousarray(np.asarray(L0_attn_x, dtype=np.float32))
    w_qkv = _stage_fm(np.ascontiguousarray(np.asarray(L0_attn_qkv_weight, np.float32).T))
    w_p = _stage_fm(np.ascontiguousarray(np.asarray(L0_attn_proj_weight, np.float32).T))
    w_r = _stage_fm(np.ascontiguousarray(np.asarray(L0_ffn_W_r, np.float32).T))
    w_k = _stage_fm(np.ascontiguousarray(np.asarray(L0_ffn_W_k, np.float32).T))
    w_v = _stage_fm(np.ascontiguousarray(np.asarray(L0_ffn_W_v, np.float32).T))
    mu_r = np.ascontiguousarray(np.asarray(L0_ffn_mu_r, np.float32).reshape(NCH, 128).T)
    mu_k = np.ascontiguousarray(np.asarray(L0_ffn_mu_k, np.float32).reshape(NCH, 128).T)
    bias = np.ascontiguousarray(np.asarray(L0_attn_proj_bias, np.float32).reshape(NCH, 128).T)
    bmask = np.kron(np.eye(4, dtype=np.float32), np.ones((32, 32), np.float32))

    nc = build()
    in_maps = []
    for b in range(B):
        xs = np.ascontiguousarray(
            x[b].T.reshape(NCH, 128, NST, ST).transpose(2, 1, 0, 3).reshape(NST, 128, NCH * ST))
        in_maps.append({
            "x_st": xs, "w_r_d": w_r, "w_k_d": w_k, "w_v_d": w_v,
            "w_qkv_d": w_qkv, "w_p_d": w_p, "mu_r_d": mu_r, "mu_k_d": mu_k,
            "bias_d": bias, "bmask_d": bmask,
        })
    res = run_bass_kernel_spmd(nc, in_maps, core_ids=list(range(B)))
    out = np.empty((B, N, C), dtype=np.float32)
    for b in range(B):
        z = np.asarray(res.results[b]["z_out"])
        out[b] = z.reshape(NST, 128, NCH, ST).transpose(0, 3, 2, 1).reshape(N, C)
    return out
